# version 06
# baseline (speedup 1.0000x reference)
import numpy as np
import ml_dtypes
import concourse.bass as bass
import concourse.mybir as mybir
from concourse.bass_utils import run_bass_kernel_spmd
from concourse.bass_isa import ReduceOp
from concourse.library_config import attn as _attn_lib

F32 = mybir.dt.float32
BF16 = mybir.dt.bfloat16
AF = mybir.ActivationFunctionType
ALU = mybir.AluOpType

MAGIC = 3 * 2.0 ** 22
NEG = -1e9

T, D, HF, EI, NH, DS = 2048, 1024, 4096, 2048, 16, 64
DP, HP, EIP = 8, 32, 16
CONVD, CVP = 2176, 17
CVP2 = 18
ZXB, ZXP = 4224, 33
CH, NCH, Q, NSUB = 512, 4, 128, 4
HSL = CH + 2
XSL = CH + 3


class Buf:
    __slots__ = ("name", "writer", "readers")

    def __init__(self, name):
        self.name, self.writer, self.readers = name, None, []


class Op:
    __slots__ = ("eng", "fn", "idx", "dma", "deps", "waits", "inc", "tag", "need",
                 "rawsrc", "engpos", "fill")

    def __init__(self, eng, fn, idx, dma, tag):
        self.eng, self.fn, self.idx, self.dma, self.tag = eng, fn, idx, dma, tag
        self.deps, self.waits, self.inc, self.need = [], {}, None, False
        self.rawsrc, self.engpos, self.fill = [], -1, False


class Sched:
    ENGS = ("sp", "gp", "pe", "dve", "act")

    def __init__(self):
        self.ops = []
        self.bufs = {}

    def B(self, name):
        b = self.bufs.get(name)
        if b is None:
            b = self.bufs[name] = Buf(name)
        return b

    def op(self, eng, fn, r=(), w=(), dma=None, tag=""):
        o = Op(eng, fn, len(self.ops), dma, tag)
        deps = {}
        for b in r:
            wtr = b.writer
            if wtr is not None and wtr.dma is None and wtr.eng == eng:
                o.rawsrc.append(wtr)
        for b in r:
            if b.writer is not None:
                deps[b.writer.idx] = b.writer
        for b in w:
            if b.writer is not None:
                deps[b.writer.idx] = b.writer
            for rd in b.readers:
                deps[rd.idx] = rd
        for b in r:
            b.readers.append(o)
        for b in w:
            b.writer, b.readers = o, []
        for d in deps.values():
            if d.dma is None and d.eng == eng:
                continue
            d.need = True
            o.deps.append(d)
        self.ops.append(o)
        return o

    def barrier(self):
        last = {}
        for o in self.ops:
            last[(o.eng, o.dma is not None)] = o
        for o in last.values():
            o.need = True
        for e in self.ENGS:
            deps = [o for o in last.values() if not (o.dma is None and o.eng == e)]
            bo = self.op(e, lambda eng: eng.nop(), tag="barrier")
            bo.deps.extend(deps)

    def finalize(self):
        pos = {}
        for o in self.ops:
            o.engpos = pos.get(o.eng, 0)
            pos[o.eng] = o.engpos + 1
        self.nfill = 0
        for o in self.ops:
            if any(w.engpos == o.engpos - 1 for w in o.rawsrc):
                o.fill = True
                self.nfill += 1
        ecount, dcount = {}, {}
        for o in self.ops:
            if o.dma is not None:
                dcount[o.dma] = dcount.get(o.dma, 0) + 16
                o.inc = ("d" + o.dma, dcount[o.dma])
            elif o.need:
                ecount[o.eng] = ecount.get(o.eng, 0) + 1
                o.inc = ("e" + o.eng, ecount[o.eng])
        for o in self.ops:
            wmap = {}
            for d in o.deps:
                assert d.idx < o.idx
                s, v = d.inc
                if wmap.get(s, -1) < v:
                    wmap[s] = v
            o.waits = wmap
        self.dma_totals = dcount

    def emit(self, eng_name, eng, sems, fill_fn=None):
        seen = {}
        for o in self.ops:
            if o.eng != eng_name:
                continue
            for s in sorted(o.waits):
                v = o.waits[s]
                if seen.get(s, -1) >= v:
                    continue
                seen[s] = v
                eng.wait_ge(sems[s], v)
            if o.fill and fill_fn is not None:
                fill_fn(eng)
            ins = o.fn(eng)
            if o.inc is not None:
                if isinstance(ins, (list, tuple)):
                    ins = ins[-1]
                assert ins is not None, o.tag
                ins.then_inc(sems[o.inc[0]], 16 if o.dma is not None else 1)


def build(phases="ABC", dump=None):
    nc = bass.Bass("TRN2", target_bir_lowering=False)
    sc = Sched()
    B = sc.B

    def din(name, shape):
        return nc.dram_tensor(name, list(shape), F32, kind="ExternalInput")

    d_u = din("u", (D, T))
    d_w1t = din("w1t", (D, HF))
    d_w2t = din("w2t", (HF, D))
    d_wipt = din("wipt", (D, ZXB))
    d_wiptdt = din("wiptdt", (D, NH))
    d_wopt = din("wopt", (EI, D))
    d_fcw = din("fcw", (HF, 3))
    d_fcb = din("fcb", (HF, 1))
    d_acw = din("acw", (CONVD + 128, 4))
    d_acb = din("acb", (CONVD + 128, 1))
    d_dtb = din("dtb", (NH, 1))
    d_alog = din("alog", (NH, 1))
    d_drow = din("drow", (1, EI))
    d_nwc = din("nwc", (1, EI))
    d_ident = din("ident", (128, 128))
    d_mnegT = din("mnegT", (128, 128))
    d_indic = din("indic", (NH, NH * 128))
    d_onesr = din("onesr", (1, T))

    d_out = nc.dram_tensor("out", [D, T], F32, kind="ExternalOutput")
    d_dbg = nc.dram_tensor("dbg", [128, 16384], F32, kind="ExternalOutput")
    d_x1 = nc.dram_tensor("xbuf1", [D, T], F32, kind="Internal")
    d_x2 = nc.dram_tensor("xbuf2", [D, T], F32, kind="Internal")
    d_xln = nc.dram_tensor("xlnb", [D, T], BF16, kind="Internal")
    d_w1q = nc.dram_tensor("w1stash", [128, DP * HF], BF16, kind="Internal")
    d_w2q = nc.dram_tensor("w2stash", [128, HP * D], BF16, kind="Internal")

    sb = {}

    def S(name, shape, dt):
        sb[name] = nc.alloc_sbuf_tensor("s_" + name, list(shape), dt)
        return sb[name]

    S("idf", (128, 128), F32)
    S("idb", (128, 128), BF16)
    S("mnegTb", (128, 128), BF16)
    S("idrepb", (128, NH * 128), BF16)
    S("ones1", (1, 128), F32)
    S("onescb", (128, 1), BF16)
    S("onescf", (128, 1), F32)
    S("eps5", (128, 1), F32)
    S("fcw", (128, HP * 3), F32)
    S("fcb", (128, HP), F32)
    S("acw", (128, CVP2 * 4), F32)
    S("acb", (128, CVP2), F32)
    S("dcol", (128, NH), F32)
    S("nwcol", (128, EIP), F32)
    S("inw2", (128, EIP), BF16)
    S("dtb", (NH, 1), F32)
    S("alog", (NH, 1), F32)
    S("negA", (NH, 1), F32)
    S("wsc", (128, 4), F32)
    S("rowsc", (1, 8), F32)
    S("DI", (128, NH * 128), BF16)
    S("Msb", (128, 128), BF16)
    S("sqpp", (128, 2 * CH), BF16)
    S("tmpc", (128, 2 * CH), BF16)
    S("cqs", (128, 2 * CH), BF16)
    S("rm0", (128, CH), F32)
    S("rm1", (128, CH), F32)
    S("bc0", (128, CH), F32)
    S("bc1", (128, CH), F32)
    S("bc2", (128, CH), F32)
    S("bc3", (128, CH), F32)
    S("rowst", (1, CH), F32)

    WREG = nc.alloc_sbuf_tensor("WREG", [128, 17920], F32)
    BIG = nc.alloc_sbuf_tensor("BIGREG", [128, 27200], F32)
    FILLSCR = nc.alloc_sbuf_tensor("fillscr", [1, 4], F32)

    def fv(off_b, p, n):
        assert off_b % 4 == 0
        return BIG.ap()[0:p, off_b // 4: off_b // 4 + n]

    def bv(off_b, p, n):
        assert off_b % 2 == 0
        return BIG.bitcast(BF16).ap()[0:p, off_b // 2: off_b // 2 + n]

    def wfv(off_b, p, n):
        return WREG.ap()[0:p, off_b // 4: off_b // 4 + n]

    def wbv(off_b, p, n):
        return WREG.bitcast(BF16).ap()[0:p, off_b // 2: off_b // 2 + n]

    w1s = lambda: wbv(0, 128, DP * HF)
    wips = lambda: wbv(0, 128, DP * ZXB)
    wkwin = lambda: wbv(67584, 128, 2 * 1024)

    hbuf = [lambda o=o: bv(o * 32896, 128, HP * HSL) for o in range(2)]
    xwin = [lambda o=o: fv(65792 + o * 16384, 128, DP * CH) for o in range(2)]
    xqb = lambda: bv(98560, 128, DP * CH)
    zg = lambda: bv(0, 128, EIP * CH)
    xbc = lambda: bv(16384, 128, CVP2 * XSL)
    cvh = lambda: bv(34944, 128, CVP2 * 256)
    Xt = lambda: bv(44160, 128, NH * 128)
    Wpp = lambda: bv(48256, 128, NH * 128)
    Wext = lambda: bv(52352, 64, NH * 128)
    zsil = lambda: bv(56448, 128, NH * 128)
    Esb = lambda: bv(60544, 128, NH * 128)
    SREG = 64640
    xlnw = lambda: bv(72832, 128, DP * CH)
    xslt = [lambda o=o: fv(81024, 128, CH) for o in range(1)]
    srows = lambda: fv(83072, NH, T)
    packL = lambda: fv(91264, 17, T)
    packR = lambda: fv(99456, 17, T)
    bdec = lambda: bv(107648, 128, 2 * DS)
    btm = lambda: bv(107904, 128, DS)
    lex = lambda: fv(108032, 17, 64)
    esq = lambda: fv(108288, 64, NSUB * NH)
    wdt = lambda: bv(108544, 128, DP * NH)
    dtr = lambda: fv(0, NH, T)
    atr = lambda: fv(8192, NH, T)
    wstgA = lambda: fv(0, 128, 4096)
    wstgB = lambda: fv(16384, 128, 4096)
    wstgQ = lambda: bv(32768, 128, 4096)
    accw = lambda: fv(40960, 128, 32)
    absq = lambda: bv(49792, 128, 4096)
    drows = lambda: fv(41600, 1, EI)

    def sview(gen, h):
        return bv(SREG, 64, 2 * NH * 128)[0:64, gen * NH * 128 + h * 128:
                                          gen * NH * 128 + (h + 1) * 128]

    psA = nc.alloc_psum_tensor("psA", [128, 2048], F32)
    psBb = lambda: psB.bitcast(BF16).ap()
    psB = nc.alloc_psum_tensor("psB", [128, 2048], F32)
    pqA = [B("psA%d" % i) for i in range(4)]
    pqB = [B("psB%d" % i) for i in range(4)]

    sems = {}
    for e in Sched.ENGS:
        sems["e" + e] = nc.alloc_semaphore("e_" + e)
    for e in ("sp", "gp", "act"):
        sems["d" + e] = nc.alloc_semaphore("d_" + e)

    def dma(q, dst, src, tag, r=(), w=(), slow=False):
        sc.op(q, lambda e, dst=dst, src=src, slow=slow: e.dma_start(
            out=dst, in_=src, allow_slow_non_contiguous=slow),
            r=r, w=w, dma=q, tag=tag)

    def colap(apv, col, count, step):
        return bass.AP(apv.tensor, apv.offset + col, [[apv.ap[0][0], apv.ap[0][1]], [step, count]])


    def part_max(tagp, src_name):
        for b in range(4):
            sc.op("pe", lambda e, b=b: e.transpose(
                psB[:, b * 128:(b + 1) * 128],
                sb[src_name][:, b * 128:(b + 1) * 128], sb["idf"][:]),
                r=(B(src_name), B("idf")), w=(pqB[0],), tag=f"{tagp}tp{b}")
        sc.op("dve", lambda e: e.tensor_reduce(
            sb["rm0"][:, 0:4], psB[:, 0:512].rearrange("j (b pp) -> j b pp", pp=128),
            mybir.AxisListType.X, ALU.max),
            r=(pqB[0], B("rm0")), w=(B("rm0"),), tag=f"{tagp}rd")
        for b in range(4):
            sc.op("gp", lambda e, b=b: e.dma_start(
                out=sb["rowst"][0:1, b * 128:(b + 1) * 128],
                in_=sb["rm0"][:, b:b + 1], allow_slow_non_contiguous=True),
                r=(B("rm0"), B("rowst")), w=(B("rowst"),), dma="gp", tag=f"{tagp}fl{b}")
        sc.op("pe", lambda e: e.matmul(psB[:, 512:1024], sb["ones1"][:], sb["rowst"][:],
                                       start=True, stop=True),
            r=(B("rowst"), B("ones1")), w=(pqB[1],), tag=f"{tagp}bc")

    def dump_hook(site, c, ap, nfree, rb=(), q="act"):
        if dump != site or (nfree > 4096 and c > 0):
            return
        dma(q, d_dbg.ap()[:, c * 4096: c * 4096 + nfree], ap,
            f"dump{site}{c}", r=rb, w=(B("dbgw"),))

    dma("sp", sb["idf"][:], d_ident.ap(), "idf", w=(B("idf"),))
    dma("gp", sb["idb"][:], d_ident.ap(), "idb", w=(B("idb"),))
    dma("gp", sb["mnegTb"][:], d_mnegT.ap(), "mnegT", w=(B("mnegTb"),))
    dma("sp", sb["fcw"][:].rearrange("p (a k) -> p a k", k=3),
        d_fcw.ap().rearrange("(a p) k -> p a k", p=128), "fcw", w=(B("fcw"),))
    dma("sp", sb["fcb"][:].rearrange("p (a k) -> p a k", k=1),
        d_fcb.ap().rearrange("(a p) k -> p a k", p=128), "fcb", w=(B("fcb"),), slow=True)
    dma("sp", sb["acw"][:].rearrange("p (a k) -> p a k", k=4),
        d_acw.ap().rearrange("(a p) k -> p a k", p=128), "acw", w=(B("acw"),))
    dma("sp", sb["acb"][:].rearrange("p (a k) -> p a k", k=1),
        d_acb.ap().rearrange("(a p) k -> p a k", p=128), "acb", w=(B("acb"),), slow=True)
    dma("sp", sb["nwcol"][:].rearrange("p (o a) -> p o a", o=1),
        d_nwc.ap().rearrange("o (a p) -> p o a", p=128), "nwc", w=(B("nwcol"),), slow=True)
    dma("sp", sb["dtb"][:], d_dtb.ap(), "dtb", w=(B("dtb"),))
    dma("sp", sb["alog"][:], d_alog.ap(), "alog", w=(B("alog"),))
    dma("sp", packR()[0:16, :], d_indic.ap(), "indic", w=(B("packind"),))
    dma("gp", wdt(), d_wiptdt.ap().rearrange("(k p) h -> p k h", p=128), "wdt", w=(B("wdt"),))
    dma("sp", drows(), d_drow.ap(), "drow", w=(B("drows"),))

    sc.op("dve", lambda e: e.memset(sb["ones1"][:], 1.0), w=(B("ones1"),), tag="o1")
    sc.op("dve", lambda e: e.memset(sb["onescb"][:], 1.0), w=(B("onescb"),), tag="ocb")
    sc.op("dve", lambda e: e.memset(sb["onescf"][:], 1.0), w=(B("onescf"),), tag="ocf")
    sc.op("dve", lambda e: e.memset(sb["eps5"][:], 1e-5), w=(B("eps5"),), tag="eps")
    dma("sp", packL()[16:17, :], d_onesr.ap(), "pk0", w=(B("packd0"),))
    dma("sp", lex()[16:17, :], d_onesr.ap()[0:1, 0:64], "lx0", w=(B("lex0"),))
    for i in range(NH):
        sc.op("dve", lambda e, i=i: e.tensor_copy(sb["idrepb"][:, i * 128:(i + 1) * 128], sb["idb"][:]),
              r=(B("idb"),), w=(B("idrepb"),), tag=f"irep{i}")

    sc.op("act", lambda e: e.activation(sb["negA"][:], sb["alog"][:], AF.Exp),
          r=(B("alog"),), w=(B("negA"),), tag="nA0")
    sc.op("dve", lambda e: e.tensor_scalar(sb["negA"][:], sb["negA"][:], -1.0, None, ALU.mult),
          r=(B("negA"),), w=(B("negA"),), tag="nA1")
    sc.op("dve", lambda e: e.tensor_mul(sb["bc0"][:, 0:EIP], sb["nwcol"][:], sb["nwcol"][:]),
          r=(B("nwcol"),), w=(B("bc0"),), tag="iw0")
    sc.op("dve", lambda e: e.reciprocal(sb["bc1"][:, 0:EIP], sb["bc0"][:, 0:EIP]),
          r=(B("bc0"),), w=(B("bc1"),), tag="iw1")
    sc.op("dve", lambda e: e.tensor_copy(sb["inw2"][:], sb["bc1"][:, 0:EIP]),
          r=(B("bc1"),), w=(B("inw2"),), tag="iw2")

    def dibc(e):
        ins = None
        for j in range(4):
            ins = e.matmul(psB[:, j * 512:(j + 1) * 512], sb["ones1"][:],
                           drows()[0:1, j * 512:(j + 1) * 512], start=True, stop=True)
        return ins
    sc.op("pe", dibc, r=(B("ones1"), B("drows")), w=tuple(pqB), tag="DIbc")
    sc.op("dve", lambda e: e.tensor_mul(
        sb["DI"][:].rearrange("p (a q) -> p a q", q=128),
        sb["idb"][:].unsqueeze(1).broadcast_to((128, NH, 128)),
        psB[:, 0:NH * 128].rearrange("p (a q) -> p a q", q=128)),
        r=tuple(pqB) + (B("idb"),), w=(B("DI"),), tag="DI")

    def prep_quant(wt_dram, kp, mfree, stash, invcol, wscol, tag, mslot, islot,
                   qn, pb, offA, offB, offQ, offacc, offabs, cpcol):
        stgA = lambda: fv(offA, 128, mfree)
        stgB = lambda: fv(offB, 128, mfree)
        stgQ = lambda: bv(offQ, 128, mfree)
        accw2 = lambda: fv(offacc, 128, kp)
        pq0, pq1 = pqB[pb // 512], pqB[pb // 512 + 1]
        stgev = [B(f"{tag}stg0"), B(f"{tag}stg1")]
        acc = B(f"{tag}acc")
        for k in range(kp):
            sg, stg = stgev[k % 2], (stgA if k % 2 == 0 else stgB)
            dma(qn, stg()[:, 0:mfree], wt_dram.ap()[k * 128:(k + 1) * 128, :],
                f"{tag}l{k}", r=(sg,), w=(sg,))
            sc.op("act", lambda e, stg=stg, k=k, mfree=mfree: e.activation(
                bv(offabs, 128, mfree)[:, 0:mfree], stg()[:, 0:mfree], AF.Abs,
                accum_out=accw2()[:, k:k + 1]),
                r=(sg, B(f"{tag}abs")), w=(B(f"{tag}abs"), acc), tag=f"{tag}a{k}")
        sc.op("pe", lambda e, kp=kp: e.matmul(psB[0:kp, pb:pb + 1], accw2()[:, 0:kp],
                                              sb["onescf"][:], start=True, stop=True),
              r=(acc, B("onescf")), w=(pq0,), tag=f"{tag}sA")
        sc.op("dve", lambda e, kp=kp: e.tensor_copy(
            sb["dcol"][0:kp, cpcol:cpcol + 1], psB[0:kp, pb:pb + 1]),
            r=(pq0,), w=(B(f"{tag}c1"),), tag=f"{tag}cp")
        sc.op("pe", lambda e, kp=kp: e.matmul(
            psB[0:1, pb + 512:pb + 513], sb["dcol"][0:kp, cpcol:cpcol + 1],
            sb["onescf"][0:kp, 0:1], start=True, stop=True),
            r=(B(f"{tag}c1"), B("onescf")), w=(pq1,), tag=f"{tag}sB")
        cnt = float(kp * 128 * mfree)
        sc.op("dve", lambda e: e.tensor_scalar(
            sb["rowsc"][0:1, mslot:mslot + 1], psB[0:1, pb + 512:pb + 513], 1.0 / cnt, 1e-5,
            ALU.mult, ALU.max), r=(pq1,), w=(B(f"{tag}m"),), tag=f"{tag}m")
        sc.op("dve", lambda e: e.reciprocal(
            sb["rowsc"][0:1, islot:islot + 1], sb["rowsc"][0:1, mslot:mslot + 1]),
            r=(B(f"{tag}m"),), w=(B(f"{tag}i"),), tag=f"{tag}i")
        sc.op("pe", lambda e: e.matmul(psB[:, pb + 512:pb + 513], sb["ones1"][:],
                                       sb["rowsc"][0:1, islot:islot + 1],
                                       start=True, stop=True),
              r=(B(f"{tag}i"), B("ones1")), w=(pq1,), tag=f"{tag}bi")
        sc.op("dve", lambda e: e.tensor_copy(sb["wsc"][:, invcol:invcol + 1],
                                             psB[:, pb + 512:pb + 513]),
              r=(pq1,), w=(B(f"{tag}ic"),), tag=f"{tag}icc")
        sc.op("pe", lambda e: e.matmul(psB[:, pb + 512:pb + 513], sb["ones1"][:],
                                       sb["rowsc"][0:1, mslot:mslot + 1],
                                       start=True, stop=True),
              r=(B(f"{tag}m"), B("ones1"), B(f"{tag}ic")), w=(pq1,), tag=f"{tag}bm")
        sc.op("dve", lambda e: e.tensor_copy(sb["wsc"][:, wscol:wscol + 1],
                                             psB[:, pb + 512:pb + 513]),
              r=(pq1,), w=(B(f"{tag}wsc"),), tag=f"{tag}wc")
        for k in range(kp):
            sg, stg = stgev[k % 2], (stgA if k % 2 == 0 else stgB)
            sg2, stg2 = stgev[(k + 1) % 2], (stgB if k % 2 == 0 else stgA)
            dma(qn, stg()[:, 0:mfree], wt_dram.ap()[k * 128:(k + 1) * 128, :],
                f"{tag}l2{k}", r=(sg,), w=(sg,))
            sc.op("dve", lambda e, stg=stg, stg2=stg2, mfree=mfree, invcol=invcol: e.tensor_scalar(
                stg2()[:, 0:mfree], stg()[:, 0:mfree], sb["wsc"][:, invcol:invcol + 1],
                MAGIC, ALU.mult, ALU.add),
                r=(sg, B(f"{tag}ic"), sg2), w=(sg2,), tag=f"{tag}q0{k}")
            sc.op("dve", lambda e, stg=stg, stg2=stg2, mfree=mfree: e.tensor_scalar(
                stg()[:, 0:mfree], stg2()[:, 0:mfree], MAGIC, 1.0, ALU.subtract, ALU.min),
                r=(sg2, sg), w=(sg,), tag=f"{tag}q1{k}")
            sc.op("dve", lambda e, stg=stg, stgQ=stgQ, mfree=mfree: e.tensor_scalar(
                stgQ()[:, 0:mfree], stg()[:, 0:mfree], -1.0, None, ALU.max),
                r=(sg, B(f"{tag}qq")), w=(B(f"{tag}qq"),), tag=f"{tag}q2{k}")
            dma("act", stash.ap()[:, k * mfree:(k + 1) * mfree], stgQ()[:, 0:mfree],
                f"{tag}st{k}", r=(B(f"{tag}qq"),), w=(B(f"{tag}sh{k}"),))

    prep_quant(d_w1t, DP, HF, d_w1q, 0, 1, "w1", 0, 1,
               "sp", 0, 0, 16384, 32768, 40960, 49792, 0)
    prep_quant(d_w2t, HP, D, d_w2q, 2, 3, "w2", 2, 3,
               "gp", 1024, 82176, 86272, 90368, 92416, 92544, 1)
    W1SHD = tuple(B(f"w1sh{k}") for k in range(DP))
    W2SHL = B(f"w2sh{HP - 1}")

    W1S = tuple(B("w1s%d" % j) for j in range(4))
    for j in range(4):
        dma("sp", wbv(j * 16384, 128, DP * HF // 4),
            d_w1q.ap()[:, j * (DP * HF // 4):(j + 1) * (DP * HF // 4)],
            f"w1l{j}", r=(W1S[j],) + W1SHD, w=(W1S[j],))

    def ffn_phase(ph, src, dst):
        HB = lambda g, o: B(f"{ph}h{g % 2}_{o}")

        def ln_stats(xbuf_ap, xbufB, c):
            for k in range(DP):
                sq = B(f"{ph}sq{k % 2}")
                sc.op("act", lambda e, k=k: e.activation(
                    sb["sqpp"][:, (k % 2) * CH:(k % 2 + 1) * CH],
                    xbuf_ap()[:, k * CH:(k + 1) * CH], AF.Square),
                    r=(xbufB, sq), w=(sq,), tag=f"{ph}sq{c}_{k}")
                sc.op("pe", lambda e, k=k: e.matmul(
                    psA[0:1, 0:CH], sb["onescb"][:], xbuf_ap()[:, k * CH:(k + 1) * CH],
                    start=(k == 0), stop=(k == DP - 1)),
                    r=(xbufB, B("onescb")), w=(pqA[0],), tag=f"{ph}mm{c}_{k}")
                sc.op("pe", lambda e, k=k: e.matmul(
                    psA[0:1, CH:2 * CH], sb["onescb"][:],
                    sb["sqpp"][:, (k % 2) * CH:(k % 2 + 1) * CH],
                    start=(k == 0), stop=(k == DP - 1)),
                    r=(sq, B("onescb")), w=(pqA[1],), tag=f"{ph}ms{c}_{k}")
            rs = B("rowst")
            sc.op("dve", lambda e: e.tensor_copy(sb["rowst"][:], psA[0:1, 0:CH]),
                  r=(pqA[0],), w=(rs,), tag=f"{ph}cm{c}")
            sc.op("pe", lambda e: e.matmul(psB[:, 0:CH], sb["ones1"][:], sb["rowst"][:],
                                           start=True, stop=True),
                  r=(rs, B("ones1")), w=(pqB[0],), tag=f"{ph}bm{c}")
            sc.op("dve", lambda e: e.tensor_scalar(sb["bc0"][:], psB[:, 0:CH], 1.0 / D, None, ALU.mult),
                  r=(pqB[0], B("bc0")), w=(B("bc0"),), tag=f"{ph}mb{c}")
            sc.op("dve", lambda e: e.tensor_copy(sb["rowst"][:], psA[0:1, CH:2 * CH]),
                  r=(pqA[1], rs), w=(rs,), tag=f"{ph}cs{c}")
            sc.op("pe", lambda e: e.matmul(psB[:, 0:CH], sb["ones1"][:], sb["rowst"][:],
                                           start=True, stop=True),
                  r=(rs, B("ones1"), B("bc0")), w=(pqB[0],), tag=f"{ph}bs{c}")
            sc.op("act", lambda e: e.activation(sb["bc2"][:], sb["bc0"][:], AF.Square),
                  r=(B("bc0"), B("bc2")), w=(B("bc2"),), tag=f"{ph}m2{c}")
            sc.op("dve", lambda e: e.scalar_tensor_tensor(
                sb["bc1"][:], psB[:, 0:CH], 1.0 / D, sb["bc2"][:], ALU.mult, ALU.subtract),
                r=(pqB[0], B("bc2"), B("bc1")), w=(B("bc1"),), tag=f"{ph}v{c}")
            sc.op("act", lambda e: e.activation(sb["bc2"][:], sb["bc1"][:], AF.Sqrt, bias=sb["eps5"][:, 0:1]),
                  r=(B("bc1"), B("bc2"), B("eps5")), w=(B("bc2"),), tag=f"{ph}sd{c}")
            sc.op("dve", lambda e: e.reciprocal(sb["bc1"][:], sb["bc2"][:]),
                  r=(B("bc2"), B("bc1")), w=(B("bc1"),), tag=f"{ph}r{c}")

        def stage_a(c):
            g = c % 2
            xw = B(f"{ph}xw{g}")
            xdep = (W2SHL,) if (ph == "A" and c == 1) else ()
            dma("sp", xwin[g]().rearrange("p (a t) -> p a t", a=DP),
                src.ap().rearrange("(a p) t -> p a t", p=128)[:, :, c * CH:(c + 1) * CH],
                f"{ph}x{c}", r=(xw,) + xdep, w=(xw,))
            xq = B(f"{ph}xq")
            for k in range(DP):
                sc.op("dve", lambda e, k=k, g=g: e.tensor_copy(
                    xqb()[:, k * CH:(k + 1) * CH], xwin[g]()[:, k * CH:(k + 1) * CH]),
                    r=(xw, xq), w=(xq,), tag=f"{ph}xb{c}_{k}")
            ln_stats(xqb, xq, f"A{c}")
            for k in range(DP):
                sc.op("dve", lambda e, k=k, g=g: e.tensor_sub(
                    sb["bc2"][:], xwin[g]()[:, k * CH:(k + 1) * CH], sb["bc0"][:]),
                    r=(xw, B("bc0"), B("bc2")), w=(B("bc2"),), tag=f"{ph}t{c}_{k}")
                sc.op("act", lambda e: e.activation(sb["bc3"][:], sb["bc2"][:], AF.Abs),
                      r=(B("bc2"), B("bc3")), w=(B("bc3"),), tag=f"{ph}ab{c}_{k}")
                if k == 0:
                    sc.op("dve", lambda e: e.tensor_copy(sb["rm0"][:], sb["bc3"][:]),
                          r=(B("bc3"), B("rm0")), w=(B("rm0"),), tag=f"{ph}rm{c}_0")
                else:
                    a, b_ = ("rm1", "rm0") if k % 2 else ("rm0", "rm1")
                    sc.op("dve", lambda e, a=a, b_=b_: e.tensor_tensor(
                        sb[a][:], sb[b_][:], sb["bc3"][:], ALU.max),
                        r=(B(b_), B("bc3"), B(a)), w=(B(a),), tag=f"{ph}rm{c}_{k}")
            part_max(f"{ph}pm{c}", "rm1")
            sc.op("dve", lambda e: e.tensor_mul(sb["bc2"][:], sb["bc1"][:], psB[:, 512:1024]),
                  r=(B("bc1"), pqB[1], B("bc2")), w=(B("bc2"),), tag=f"{ph}rx{c}")
            sc.op("dve", lambda e: e.tensor_scalar(sb["bc3"][:], sb["bc2"][:], 1e-5, None, ALU.max),
                  r=(B("bc2"), B("bc3")), w=(B("bc3"),), tag=f"{ph}mg{c}")
            sc.op("dve", lambda e: e.reciprocal(sb["bc2"][:], sb["bc3"][:]),
                  r=(B("bc3"), B("bc2")), w=(B("bc2"),), tag=f"{ph}ig{c}")
            sc.op("dve", lambda e: e.tensor_mul(sb["rm0"][:], sb["bc1"][:], sb["bc2"][:]),
                  r=(B("bc1"), B("bc2"), B("rm0")), w=(B("rm0"),), tag=f"{ph}a0{c}")
            sc.op("dve", lambda e: e.tensor_scalar(sb["bc1"][:], sb["rm0"][:], 127.0, None, ALU.mult),
                  r=(B("rm0"), B("bc1")), w=(B("bc1"),), tag=f"{ph}a1{c}")
            sc.op("dve", lambda e: e.tensor_scalar(sb["rm1"][:], sb["bc3"][:], 1.0 / 127, None, ALU.mult),
                  r=(B("bc3"), B("rm1")), w=(B("rm1"),), tag=f"{ph}be{c}")
            for k in range(DP):
                sc.op("dve", lambda e, k=k, g=g: e.tensor_sub(
                    sb["bc2"][:], xwin[g]()[:, k * CH:(k + 1) * CH], sb["bc0"][:]),
                    r=(xw, B("bc0"), B("bc2")), w=(B("bc2"),), tag=f"{ph}qs{c}_{k}")
                sc.op("dve", lambda e: e.tensor_mul(sb["rm0"][:], sb["bc2"][:], sb["bc1"][:]),
                      r=(B("bc2"), B("bc1"), B("rm0")), w=(B("rm0"),), tag=f"{ph}qm{c}_{k}")
                sc.op("dve", lambda e: e.tensor_scalar(
                    sb["bc2"][:], sb["rm0"][:], MAGIC, MAGIC, ALU.add, ALU.subtract),
                    r=(B("rm0"), B("bc2")), w=(B("bc2"),), tag=f"{ph}qr{c}_{k}")
                sc.op("dve", lambda e, k=k: e.tensor_mul(
                    xqb()[:, k * CH:(k + 1) * CH], sb["bc2"][:], sb["rm1"][:]),
                    r=(B("bc2"), B("rm1"), xq), w=(xq,), tag=f"{ph}qd{c}_{k}")
            if c == 0:
                dump_hook("wsc", c, sb["wsc"][:], 4, rb=(B("w1wsc"), B("w2wsc")), q="act")
                dump_hook("w1s", c, w1s()[:, 0:4096], 4096, rb=W1S, q="gp")
            dump_hook("xq", c, xqb()[:, 0:4096 if False else DP * CH][:, 0:4096], 4096,
                      rb=(xq,), q="gp")
            dump_hook("alpha", c, sb["bc1"][:], CH, rb=(B("bc1"),), q="act")
            dump_hook("beta", c, sb["rm1"][:], CH, rb=(B("rm1"),), q="act")
            dump_hook("mb", c, sb["bc0"][:], CH, rb=(B("bc0"),), q="act")
            dump_hook("rb", c, sb["bc1x" if False else "bc1"][:], CH, rb=(B("bc1"),), q="act")
            for grp in range(8):
                ps, pq = (psA, pqA) if grp % 2 == 0 else (psB, pqB)
                for oo in range(4):
                    o = grp * 4 + oo
                    for k in range(DP):
                        sc.op("pe", lambda e, ps=ps, oo=oo, o=o, k=k: e.matmul(
                            ps[:, oo * CH:(oo + 1) * CH],
                            w1s()[:, (k * HF + o * 128):(k * HF + o * 128 + 128)],
                            xqb()[:, k * CH:(k + 1) * CH],
                            start=(k == 0), stop=(k == DP - 1)),
                            r=(xq,) + W1S, w=(pq[oo],), tag=f"{ph}w1_{c}_{o}_{k}")
                for oo in range(4):
                    o = grp * 4 + oo
                    pq = (pqA if grp % 2 == 0 else pqB)[oo]
                    sc.op("act", lambda e, grp=grp, oo=oo, o=o, g=g: e.activation(
                        hbuf[g]()[:, o * HSL + 1: o * HSL + 1 + CH],
                        (psA if grp % 2 == 0 else psB)[:, oo * CH:(oo + 1) * CH],
                        AF.Copy, scale=sb["wsc"][:, 1:2]),
                        r=(pq, B("w1wsc"), HB(g, o)), w=(HB(g, o),), tag=f"{ph}he{c}_{o}")
            allh = tuple(HB(g, o) for o in range(HP))
            allhp = tuple(HB(1 - g, o) for o in range(HP))
            if c == 0:
                hzdep = (B("w1stg0"), B("w1stg1")) if ph == "A" else ()
                sc.op("dve", lambda e, g=g: e.memset(colap(hbuf[g](), 0, HP, HSL), 0.0),
                      r=allh + hzdep, w=allh, tag=f"{ph}hz{c}")
            else:
                sc.op("dve", lambda e, g=g: e.tensor_copy(
                    colap(hbuf[g](), 0, HP, HSL), colap(hbuf[1 - g](), CH, HP, HSL)),
                    r=allhp + allh, w=allh, tag=f"{ph}hl{c}")
                sc.op("dve", lambda e, g=g: e.tensor_copy(
                    colap(hbuf[1 - g](), CH + 1, HP, HSL), colap(hbuf[g](), 1, HP, HSL)),
                    r=allh + allhp, w=allhp, tag=f"{ph}hr{c}")
            if c == NCH - 1:
                sc.op("dve", lambda e, g=g: e.memset(colap(hbuf[g](), CH + 1, HP, HSL), 0.0),
                      r=allh, w=allh, tag=f"{ph}hz2{c}")

        def stage_b(p):
            g = p % 2
            xw = B(f"{ph}xw{g}")
            hs = hbuf[g]
            dump_hook("h", p, hs()[:, 0:8192], 8192,
                      rb=tuple(HB(g, o) for o in range(HP)), q="gp")
            for o in range(HP):
                base = o * HSL
                sc.op("dve", lambda e, base=base, o=o: e.tensor_scalar(
                    sb["tmpc"][:, 0:CH], hs()[:, base:base + CH],
                    sb["fcw"][:, 3 * o:3 * o + 1], None, ALU.mult),
                    r=(HB(g, o), B("fcw"), B("tm0")), w=(B("tm0"),), tag=f"{ph}c0{p}_{o}")
                sc.op("dve", lambda e, base=base, o=o: e.scalar_tensor_tensor(
                    sb["tmpc"][:, CH:2 * CH], hs()[:, base + 1:base + 1 + CH],
                    sb["fcw"][:, 3 * o + 1:3 * o + 2], sb["tmpc"][:, 0:CH],
                    ALU.mult, ALU.add),
                    r=(HB(g, o), B("tm0"), B("fcw"), B("tm1")), w=(B("tm1"),), tag=f"{ph}c1{p}_{o}")
                sc.op("dve", lambda e, base=base, o=o: e.scalar_tensor_tensor(
                    sb["tmpc"][:, 0:CH], hs()[:, base + 2:base + 2 + CH],
                    sb["fcw"][:, 3 * o + 2:3 * o + 3], sb["tmpc"][:, CH:2 * CH],
                    ALU.mult, ALU.add),
                    r=(HB(g, o), B("tm1"), B("fcw"), B("tm0")), w=(B("tm0"),), tag=f"{ph}c2{p}_{o}")
                sc.op("act", lambda e, base=base, o=o: e.activation(
                    hs()[:, base + 1:base + 1 + CH], sb["tmpc"][:, 0:CH], AF.Silu,
                    bias=sb["fcb"][:, o:o + 1]),
                    r=(B("tm0"), B("fcb"), HB(g, o)), w=(HB(g, o),), tag=f"{ph}si{p}_{o}")
                sc.op("act", lambda e, base=base: e.activation(
                    sb["bc3"][:], hs()[:, base + 1:base + 1 + CH], AF.Abs),
                    r=(HB(g, o), B("bc3")), w=(B("bc3"),), tag=f"{ph}cab{p}_{o}")
                if o == 0:
                    sc.op("dve", lambda e: e.tensor_copy(sb["rm0"][:], sb["bc3"][:]),
                          r=(B("bc3"), B("rm0")), w=(B("rm0"),), tag=f"{ph}cr{p}_0")
                else:
                    a, b_ = ("rm1", "rm0") if o % 2 else ("rm0", "rm1")
                    sc.op("dve", lambda e, a=a, b_=b_: e.tensor_tensor(
                        sb[a][:], sb[b_][:], sb["bc3"][:], ALU.max),
                        r=(B(b_), B("bc3"), B(a)), w=(B(a),), tag=f"{ph}cr{p}_{o}")
            part_max(f"{ph}cm{p}", "rm1")
            sc.op("dve", lambda e: e.tensor_scalar(sb["bc0"][:], psB[:, 512:1024], 1e-5, None, ALU.max),
                  r=(pqB[1], B("bc0")), w=(B("bc0"),), tag=f"{ph}cg{p}")
            sc.op("dve", lambda e: e.reciprocal(sb["bc2"][:], sb["bc0"][:]),
                  r=(B("bc0"), B("bc2")), w=(B("bc2"),), tag=f"{ph}ci{p}")
            sc.op("dve", lambda e: e.tensor_scalar(sb["bc1"][:], sb["bc2"][:], 127.0, None, ALU.mult),
                  r=(B("bc2"), B("bc1")), w=(B("bc1"),), tag=f"{ph}ca{p}")
            sc.op("dve", lambda e: e.tensor_scalar(sb["bc2"][:], sb["bc0"][:], 0.5 / 127.0,
                                                   sb["wsc"][:, 3:4], ALU.mult, ALU.mult),
                  r=(B("bc0"), B("w2wsc"), B("bc2")), w=(B("bc2"),), tag=f"{ph}cgm{p}")
            for k in range(HP):
                kb = B(f"{ph}cq{k % 2}")
                rmk = "rm0" if k % 2 == 0 else "rm1"
                sc.op("dve", lambda e, k=k, rmk=rmk: e.tensor_mul(
                    sb[rmk][:], hs()[:, k * HSL + 1:k * HSL + 1 + CH], sb["bc1"][:]),
                    r=(HB(g, k), B("bc1"), B(rmk)), w=(B(rmk),), tag=f"{ph}q0{p}_{k}")
                sc.op("dve", lambda e, k=k, rmk=rmk: e.tensor_scalar(
                    sb["cqs"][:, (k % 2) * CH:(k % 2 + 1) * CH], sb[rmk][:],
                    MAGIC, MAGIC, ALU.add, ALU.subtract),
                    r=(B(rmk), kb), w=(kb,), tag=f"{ph}q1{p}_{k}")
                wk = B(f"{ph}wk{k % 2}")
                wkdep = (W2SHL,) if (ph == "A" and p == 0) else ()
                dma("sp", wkwin()[:, (k % 2) * 1024:(k % 2 + 1) * 1024],
                    d_w2q.ap()[:, k * 1024:(k + 1) * 1024], f"{ph}w2l{p}_{k}",
                    r=(wk,) + wkdep, w=(wk,))
                for o in range(8):
                    ps = psA if o < 4 else psB
                    pq = pqA[o % 4] if o < 4 else pqB[o % 4]
                    sc.op("pe", lambda e, ps=ps, o=o, k=k: e.matmul(
                        ps[:, (o % 4) * CH:(o % 4 + 1) * CH],
                        wkwin()[:, (k % 2) * 1024 + o * 128:(k % 2) * 1024 + o * 128 + 128],
                        sb["cqs"][:, (k % 2) * CH:(k % 2 + 1) * CH],
                        start=(k == 0), stop=(k == HP - 1)),
                        r=(kb, wk), w=(pq,), tag=f"{ph}w2_{p}_{o}_{k}")
            dump_hook("c", p, hs()[:, 0:8192], 8192,
                      rb=tuple(HB(g, o) for o in range(HP)), q="gp")
            dump_hook("gam", p, sb["bc2"][:], CH, rb=(B("bc2"),), q="act")
            for o in range(8):
                ps = psA if o < 4 else psB
                pq = pqA[o % 4] if o < 4 else pqB[o % 4]
                rmk = "rm0" if o % 2 == 0 else "rm1"
                ot = "bc3" if o % 2 == 0 else "bc0"
                sc.op("dve", lambda e, ps=ps, o=o, rmk=rmk: e.tensor_mul(
                    sb[rmk][:], ps[:, (o % 4) * CH:(o % 4 + 1) * CH], sb["bc2"][:]),
                    r=(pq, B("bc2"), B(rmk)), w=(B(rmk),), tag=f"{ph}e0{p}_{o}")
                sc.op("dve", lambda e, o=o, g=g, rmk=rmk, ot=ot: e.scalar_tensor_tensor(
                    sb[ot][:], sb[rmk][:], 0.0,
                    xwin[g]()[:, o * CH:(o + 1) * CH], ALU.bypass, ALU.add),
                    r=(B(rmk), xw, B(ot)), w=(B(ot),), tag=f"{ph}e1{p}_{o}")
                dma("act", dst.ap()[o * 128:(o + 1) * 128, p * CH:(p + 1) * CH],
                    sb[ot][:], f"{ph}st{p}_{o}", r=(B(ot),), w=(B(f"{ph}d{p}_{o}"),))

        for c in range(NCH + 1):
            if c < NCH:
                stage_a(c)
            if c >= 1:
                stage_b(c - 1)

    ffn_phase("A", d_u, d_x1)
    sc.barrier()
    if phases == "A":
        dma("sp", d_out.ap(), d_x1.ap(), "cpout", r=(), w=())
        sc.finalize()
        _emit(nc, sc, sems, FILLSCR)
        return nc

    mkA = B("postA")
    sc.op("dve", lambda e: e.memset(sb["eps5"][0:1, 0:1], 1e-5), w=(mkA,), tag="postA")
    dma("sp", packR()[0:16, :], d_indic.ap(), "indicR", r=(mkA,), w=(B("packind"),))
    dma("sp", packL()[16:17, :], d_onesr.ap(), "pk0R", r=(mkA,), w=(B("packd0"),))
    WIPS = tuple(B("wips%d" % k) for k in range(DP))
    for k in range(DP):
        dma("gp", wbv(k * (ZXB * 2), 128, ZXB),
            d_wipt.ap()[k * 128:(k + 1) * 128, :], f"ipl{k}", r=(mkA,), w=(WIPS[k],))

    xlnw2 = lambda: bv(34944, 128, DP * CH)
    tmf = lambda: sb["tmpc"].bitcast(F32).ap()[0:128, 0:CH]
    TMB = (B("tm0"), B("tm1"))
    for c in range(NCH):
        xl = B("Bxln0")
        xlv = xlnw
        for k in range(DP):
            dma("gp", xlv()[:, k * CH:(k + 1) * CH],
                d_x1.ap()[k * 128:(k + 1) * 128, c * CH:(c + 1) * CH],
                f"B0b{c}_{k}", r=(xl, B(f"Ad{c}_{k}")), w=(xl,))
        for k in range(DP):
            sq = B(f"B0q{k % 2}")
            sc.op("act", lambda e, k=k, xlv=xlv: e.activation(
                sb["sqpp"][:, (k % 2) * CH:(k % 2 + 1) * CH],
                xlv()[:, k * CH:(k + 1) * CH], AF.Square),
                r=(xl, sq), w=(sq,), tag=f"B0s{c}_{k}")
            sc.op("pe", lambda e, k=k, xlv=xlv: e.matmul(
                psA[0:1, 0:CH], sb["onescb"][:], xlv()[:, k * CH:(k + 1) * CH],
                start=(k == 0), stop=(k == DP - 1)),
                r=(xl, B("onescb")), w=(pqA[0],), tag=f"B0m{c}_{k}")
            sc.op("pe", lambda e, k=k: e.matmul(
                psA[0:1, CH:2 * CH], sb["onescb"][:],
                sb["sqpp"][:, (k % 2) * CH:(k % 2 + 1) * CH],
                start=(k == 0), stop=(k == DP - 1)),
                r=(sq, B("onescb")), w=(pqA[1],), tag=f"B0ss{c}_{k}")
        rs = B("rowst")
        sc.op("dve", lambda e: e.tensor_copy(sb["rowst"][:], psA[0:1, 0:CH]),
              r=(pqA[0], rs), w=(rs,), tag=f"B0c{c}")
        sc.op("pe", lambda e: e.matmul(psB[:, 0:CH], sb["ones1"][:], sb["rowst"][:],
                                       start=True, stop=True),
              r=(rs, B("ones1")), w=(pqB[0],), tag=f"B0bm{c}")
        sc.op("dve", lambda e: e.tensor_scalar(sb["bc0"][:], psB[:, 0:CH], 1.0 / D, None, ALU.mult),
              r=(pqB[0], B("bc0")), w=(B("bc0"),), tag=f"B0mb{c}")
        sc.op("dve", lambda e: e.tensor_copy(sb["rowst"][:], psA[0:1, CH:2 * CH]),
              r=(pqA[1], rs), w=(rs,), tag=f"B0c2{c}")
        sc.op("pe", lambda e: e.matmul(psB[:, 0:CH], sb["ones1"][:], sb["rowst"][:],
                                       start=True, stop=True),
              r=(rs, B("ones1"), B("bc0")), w=(pqB[0],), tag=f"B0bs{c}")
        sc.op("act", lambda e: e.activation(sb["bc2"][:], sb["bc0"][:], AF.Square),
              r=(B("bc0"), B("bc2")), w=(B("bc2"),), tag=f"B0m2{c}")
        sc.op("dve", lambda e: e.scalar_tensor_tensor(
            sb["bc1"][:], psB[:, 0:CH], 1.0 / D, sb["bc2"][:], ALU.mult, ALU.subtract),
            r=(pqB[0], B("bc2"), B("bc1")), w=(B("bc1"),), tag=f"B0v{c}")
        sc.op("act", lambda e: e.activation(sb["bc2"][:], sb["bc1"][:], AF.Sqrt, bias=sb["eps5"][:, 0:1]),
              r=(B("bc1"), B("bc2"), B("eps5")), w=(B("bc2"),), tag=f"B0sd{c}")
        sc.op("dve", lambda e: e.reciprocal(sb["bc1"][:], sb["bc2"][:]),
              r=(B("bc2"), B("bc1")), w=(B("bc1"),), tag=f"B0r{c}")
        sc.op("dve", lambda e: e.tensor_mul(sb["bc2"][:], sb["bc0"][:], sb["bc1"][:]),
              r=(B("bc0"), B("bc1"), B("bc2")), w=(B("bc2"),), tag=f"B0mr{c}")
        for k in range(DP):
            xsv = xslt[0]
            xsb = (B("Bxs0"),)
            dma("sp", xsv(), d_x1.ap()[k * 128:(k + 1) * 128, c * CH:(c + 1) * CH],
                f"B0x{c}_{k}", r=xsb, w=xsb)
            rmk = "rm0" if k % 2 == 0 else "rm1"
            sc.op("dve", lambda e, k=k, rmk=rmk, xsv=xsv: e.tensor_mul(
                sb[rmk][:], xsv(), sb["bc1"][:]),
                r=xsb + (B("bc1"), B(rmk)), w=(B(rmk),), tag=f"B0l0{c}_{k}")
            sc.op("dve", lambda e, k=k, rmk=rmk, xlv=xlv: e.tensor_sub(
                xlv()[:, k * CH:(k + 1) * CH], sb[rmk][:], sb["bc2"][:]),
                r=(B(rmk), B("bc2"), xl), w=(xl,), tag=f"B0l1{c}_{k}")
            sc.op("pe", lambda e, k=k, xlv=xlv: e.matmul(
                psA[0:NH, 1024:1024 + CH], wdt()[:, k * NH:(k + 1) * NH],
                xlv()[:, k * CH:(k + 1) * CH], start=(k == 0), stop=(k == DP - 1)),
                r=(xl, B("wdt")), w=(pqA[2],), tag=f"B0d{c}_{k}")
        dma("act", d_xln.ap().rearrange("(a p) t -> p a t", p=128)[:, :, c * CH:(c + 1) * CH],
            xlv().rearrange("p (a t) -> p a t", a=DP), f"B0w{c}",
            r=(xl,), w=(B(f"B0st{c}"),))
        sc.op("act", lambda e: e.activation(sb["bc0"][0:NH, :], psA[0:NH, 1024:1024 + CH],
                                            AF.Abs, bias=sb["dtb"][:]),
              r=(pqA[2], B("dtb"), B("bc0")), w=(B("bc0"),), tag=f"B0A{c}")
        sc.op("act", lambda e: e.activation(sb["bc1"][0:NH, :], sb["bc0"][0:NH, :],
                                            AF.Exp, scale=-1.0),
              r=(B("bc0"), B("bc1")), w=(B("bc1"),), tag=f"B0E{c}")
        sc.op("act", lambda e: e.activation(sb["bc0"][0:NH, :], sb["bc1"][0:NH, :],
                                            AF.Ln, bias=sb["onescf"][0:NH, 0:1]),
              r=(B("bc1"), B("bc0"), B("onescf")), w=(B("bc0"),), tag=f"B0L{c}")
        sc.op("act", lambda e: e.activation(sb["bc1"][0:NH, :], psA[0:NH, 1024:1024 + CH],
                                            AF.Relu, bias=sb["dtb"][:]),
              r=(pqA[2], B("dtb"), B("bc1")), w=(B("bc1"),), tag=f"B0R{c}")
        sc.op("dve", lambda e, c=c: e.tensor_add(dtr()[:, c * CH:(c + 1) * CH],
                                                 sb["bc0"][0:NH, :], sb["bc1"][0:NH, :]),
              r=(B("bc0"), B("bc1"), B("dtr")), w=(B("dtr"),), tag=f"B0T{c}")

    sc.op("dve", lambda e: e.tensor_scalar(atr()[:, :], dtr()[:, :], sb["negA"][0:NH, 0:1],
                                           None, ALU.mult),
          r=(B("dtr"), B("negA"), B("atr")), w=(B("atr"),), tag="Ba")
    sc.op("dve", lambda e: e.memset(packL()[0:16, :], 0.0), w=(B("packd"),), tag="Bz")
    sc.op("dve", lambda e: e.tensor_tensor_scan(
        srows()[:, :], atr()[:, :], packL()[0:16, :], 0.0, ALU.add, ALU.add),
        r=(B("atr"), B("packd"), B("srows")), w=(B("srows"),), tag="Bscan")
    sc.op("act", lambda e: e.activation(atr()[:, :], dtr()[:, :], AF.Ln),
          r=(B("dtr"), B("atr")), w=(B("atr"),), tag="Bln")
    sc.op("dve", lambda e: e.tensor_sub(packL()[0:16, :], atr()[:, :], srows()[:, :]),
          r=(B("atr"), B("srows"), B("packd")), w=(B("packd"),), tag="Bd17")
    sc.op("dve", lambda e: e.memset(bv(SREG, 64, NH * 128), 0.0),
          w=(B("S0"),), tag="BS0")
    if dump == "B1":
        dma("act", d_dbg.ap()[0:16, 0:2048], dtr(), "dmpd", r=(B("dtr"),), w=(B("dbgw"),))
        dma("act", d_dbg.ap()[0:16, 2048:4096], srows()[0:16, :], "dmps",
            r=(B("srows"),), w=(B("dbgw"),))
        dma("act", d_dbg.ap()[0:17, 4096:6144], packL(), "dmpl",
            r=(B("packd"), B("packd0")), w=(B("dbgw"),))
    sc.barrier()

    SGEN = [0]

    def mamba_chunk(c):
        xl = B("Bxln0")
        dma("gp", xlnw().rearrange("p (a t) -> p a t", a=DP),
            d_xln.ap().rearrange("(a p) t -> p a t", p=128)[:, :, c * CH:(c + 1) * CH],
            f"B1xl{c}", r=(xl, B(f"B0st{c}")), w=(xl,))
        ZGS = [B(f"B1z{o}") for o in range(EIP)]
        XBS = [B(f"B1b{j}") for j in range(CVP2)]
        for o in range(ZXP + 1):
            pq = (pqA + pqB)[o % 8]
            ps = psA if (o % 8) < 4 else psB
            slt = (o % 8) % 4
            half_tile = o >= 32
            np_ = 64 if half_tile else 128
            wcol = (4096 + (o - 32) * 64) if half_tile else o * 128
            for k in range(DP):
                sc.op("pe", lambda e, ps=ps, slt=slt, k=k, np_=np_, wcol=wcol: e.matmul(
                    ps[0:np_, slt * CH:(slt + 1) * CH],
                    wips()[:, (k * ZXB + wcol):(k * ZXB + wcol + np_)],
                    xlnw()[:, k * CH:(k + 1) * CH],
                    start=(k == 0), stop=(k == DP - 1)),
                    r=(xl,) + WIPS, w=(pq,), tag=f"B1i{c}_{o}_{k}")
            if o < EIP:
                sc.op("act", lambda e, ps=ps, slt=slt, o=o: e.activation(
                    zg()[:, o * CH:(o + 1) * CH], ps[:, slt * CH:(slt + 1) * CH], AF.Copy),
                    r=(pq, ZGS[o]), w=(ZGS[o],), tag=f"B1ze{c}_{o}")
            else:
                j = o - EIP
                if c == 0:
                    sc.op("dve", lambda e, j=j, np_=np_: e.memset(
                        xbc()[0:np_, j * XSL:j * XSL + 3], 0.0),
                        r=(XBS[j],), w=(XBS[j],), tag=f"B1t{c}_{j}")
                else:
                    sc.op("dve", lambda e, j=j, np_=np_: e.tensor_copy(
                        xbc()[0:np_, j * XSL:j * XSL + 3],
                        xbc()[0:np_, j * XSL + CH:j * XSL + CH + 3]),
                        r=(XBS[j],), w=(XBS[j],), tag=f"B1t{c}_{j}")
                sc.op("act", lambda e, ps=ps, slt=slt, j=j, np_=np_: e.activation(
                    xbc()[0:np_, j * XSL + 3:j * XSL + 3 + CH],
                    ps[0:np_, slt * CH:(slt + 1) * CH], AF.Copy),
                    r=(pq, XBS[j]), w=(XBS[j],), tag=f"B1xe{c}_{j}")
        if dump == "B1" and c == 0:
            dma("gp", d_dbg.ap()[:, 6144:6144 + CVP2 * XSL], xbc(), "dmpx",
                r=tuple(XBS), w=(B("dbgw"),))
        EQ = B("B1eq")

        def esq_build(e, c=c):
            sr = srows()
            pst = sr.ap[0][0]
            t, off = sr.tensor, sr.offset
            lastap = bass.AP(t, off + c * CH + Q - 1, [[pst, NH], [Q, NSUB]])
            if c == 0:
                e.tensor_sub(sb["bc0"][0:NH, 1:NSUB],
                             bass.AP(t, off + 2 * Q - 1, [[pst, NH], [Q, NSUB - 1]]),
                             bass.AP(t, off + Q - 1, [[pst, NH], [Q, NSUB - 1]]))
                return e.tensor_copy(sb["bc0"][0:NH, 0:1],
                                     bass.AP(t, off + Q - 1, [[pst, NH], [1, 1]]))
            prevap = bass.AP(t, off + c * CH - 1, [[pst, NH], [Q, NSUB]])
            return e.tensor_sub(sb["bc0"][0:NH, 0:NSUB], lastap, prevap)
        sc.op("dve", esq_build, r=(B("srows"), B("bc0")), w=(B("bc0"),), tag=f"B1q0{c}")
        sc.op("act", lambda e: e.activation(sb["bc1"][0:NH, 0:NSUB], sb["bc0"][0:NH, 0:NSUB], AF.Exp),
              r=(B("bc0"), B("bc1")), w=(B("bc1"),), tag=f"B1q1{c}")
        sc.op("gp", lambda e: e.dma_start(out=sb["rowst"][0:1, 0:NH * NSUB],
                                          in_=sb["bc1"][0:NH, 0:NSUB]),
              r=(B("bc1"), B("rowst")), w=(B("rowst"),), dma="gp", tag=f"B1qf{c}")
        sc.op("pe", lambda e: e.matmul(psB[0:64, 1024:1024 + NH * NSUB], sb["ones1"][0:1, 0:64],
                                       sb["rowst"][0:1, 0:NH * NSUB], start=True, stop=True),
              r=(B("rowst"), B("ones1")), w=(pqB[2],), tag=f"B1qb{c}")
        sc.op("dve", lambda e: e.tensor_copy(esq(), psB[0:64, 1024:1024 + NH * NSUB]),
              r=(pqB[2],), w=(EQ,), tag=f"B1qc{c}")

        for q in range(NSUB):
            half, qh = q // 2, (q % 2) * 128
            CVB = B("B1cv")
            if q % 2 == 0:
                for j in range(CVP2):
                    np_ = 128 if j < 16 else 64
                    base = j * XSL + half * 256
                    sc.op("dve", lambda e, base=base, j=j, np_=np_: e.tensor_scalar(
                        sb["tmpc"][0:np_, 0:256], xbc()[0:np_, base:base + 256],
                        sb["acw"][0:np_, 4 * j:4 * j + 1], None, ALU.mult),
                        r=(XBS[j], B("acw"), B("tm0")), w=(B("tm0"),), tag=f"B1v0{c}_{q}_{j}")
                    for kk in (1, 2, 3):
                        src_t = "tm0" if kk % 2 == 1 else "tm1"
                        dst_t = "tm1" if kk % 2 == 1 else "tm0"
                        sc.op("dve", lambda e, base=base, j=j, kk=kk, np_=np_: e.scalar_tensor_tensor(
                            sb["tmpc"][0:np_, (kk % 2) * CH:(kk % 2) * CH + 256],
                            xbc()[0:np_, base + kk:base + kk + 256],
                            sb["acw"][0:np_, 4 * j + kk:4 * j + kk + 1],
                            sb["tmpc"][0:np_, ((kk + 1) % 2) * CH:((kk + 1) % 2) * CH + 256],
                            ALU.mult, ALU.add),
                            r=(XBS[j], B(src_t), B("acw"), B(dst_t)), w=(B(dst_t),),
                            tag=f"B1v{kk}{c}_{q}_{j}")
                    cvdep = (B(f"B0st{NCH - 1}"),) if c == 0 else ()
                    sc.op("act", lambda e, j=j, np_=np_: e.activation(
                        cvh()[0:np_, j * 256:(j + 1) * 256], sb["tmpc"][0:np_, CH:CH + 256],
                        AF.Silu, bias=sb["acb"][0:np_, j:j + 1]),
                        r=(B("tm1"), B("acb"), CVB) + cvdep, w=(CVB,), tag=f"B1vs{c}_{q}_{j}")
            sc.op("pe", lambda e, qh=qh: e.matmul(
                psB[:, 0:128], cvh()[0:64, 16 * 256 + qh:16 * 256 + qh + 128],
                cvh()[0:64, 17 * 256 + qh:17 * 256 + qh + 128], start=True, stop=True),
                r=(CVB,), w=(pqB[0],), tag=f"B1M{c}_{q}")
            sc.op("dve", lambda e: e.tensor_copy(sb["Msb"][:], psB[:, 0:128]),
                  r=(pqB[0], B("Msb")), w=(B("Msb"),), tag=f"B1Me{c}_{q}")
            sc.op("pe", lambda e, qh=qh: e.transpose(
                psBb()[0:128, 256:320], cvh()[0:64, 16 * 256 + qh:16 * 256 + qh + 128],
                sb["idb"][0:64, 0:64]),
                r=(CVB, B("idb")), w=(pqB[0],), tag=f"B1bt{c}_{q}")
            sc.op("dve", lambda e: e.tensor_copy(btm(), psBb()[0:128, 256:320]),
                  r=(pqB[0], B("btm")), w=(B("btm"),), tag=f"B1be{c}_{q}")
            if c == 0 and q == 0:
                sc.op("dve", lambda e: e.memset(lex()[0:16, :], 0.0),
                      r=(B("lexr"),), w=(B("lexr"),), tag="B1x00")
            else:
                def lexf(e, c=c, q=q):
                    sr = srows()
                    col = bass.AP(sr.tensor, sr.offset + c * CH + q * Q - 1,
                                  [[sr.ap[0][0], NH], [0, 64]])
                    return e.tensor_scalar(lex()[0:16, :], col, -1.0, None, ALU.mult)
                sc.op("dve", lexf, r=(B("srows"), B("lexr")), w=(B("lexr"),), tag=f"B1x{c}_{q}")

            def sflat(e, c=c, q=q):
                sr = srows()
                src = bass.AP(sr.tensor, sr.offset + c * CH + q * Q, [[sr.ap[0][0], NH], [1, Q]])
                return e.dma_start(out=packR()[16:17, :], in_=src)
            sc.op("gp", sflat, r=(B("srows"), B("packs")), w=(B("packs"),), dma="gp",
                  tag=f"B1f{c}_{q}")
            for j in range(4):
                sc.op("pe", lambda e, j=j, c=c, q=q: e.matmul(
                    psA[:, j * 512:(j + 1) * 512],
                    packL()[0:17, c * CH + q * Q:c * CH + q * Q + Q],
                    packR()[0:17, j * 512:(j + 1) * 512], start=True, stop=False),
                    r=(B("packs"), B("packd"), B("packind"), B("packd0")),
                    w=(pqA[j],), tag=f"B1E{c}_{q}_{j}")
                sc.op("pe", lambda e, j=j: e.matmul(
                    psA[:, j * 512:(j + 1) * 512], sb["mnegTb"][:],
                    sb["idrepb"][:, j * 512:(j + 1) * 512], start=False, stop=True),
                    r=(B("mnegTb"), B("idrepb")), w=(pqA[j],), tag=f"B1G{c}_{q}_{j}")
            sc.op("act", lambda e: e.activation(Esb(), psA[:, 0:2048], AF.Exp),
                  r=tuple(pqA) + (B("Esb"),), w=(B("Esb"),), tag=f"B1X{c}_{q}")
            sc.op("act", lambda e: e.activation(
                sb["dcol"][:], bass.AP(psA.ap().tensor, psA.ap().offset + 127,
                                       [[psA.ap().ap[0][0], 128], [128, NH]]), AF.Exp),
                r=tuple(pqA) + (B("dcol"),), w=(B("dcol"),), tag=f"B1dc{c}_{q}")
            for j in range(4):
                sc.op("pe", lambda e, j=j: e.matmul(
                    psB[0:64, j * 512:(j + 1) * 512], lex(),
                    packR()[0:17, j * 512:(j + 1) * 512], start=True, stop=True),
                    r=(B("packs"), B("packind"), B("lexr"), B("lex0")),
                    w=(pqB[j],), tag=f"B1H{c}_{q}_{j}")
            sc.op("act", lambda e: e.activation(psA[0:64, 0:2048], psB[0:64, 0:2048], AF.Exp),
                  r=tuple(pqB) + tuple(pqA), w=tuple(pqA), tag=f"B1Y{c}_{q}")
            sc.op("dve", lambda e: e.tensor_mul(
                Wpp().rearrange("p (a t) -> p a t", t=128),
                Esb().rearrange("p (a t) -> p a t", t=128),
                sb["Msb"][:].unsqueeze(1).broadcast_to((128, NH, 128))),
                r=(B("Esb"), B("Msb"), B("Wpp")), w=(B("Wpp"),), tag=f"B1W{c}_{q}")
            sc.op("dve", lambda e, qh=qh: e.tensor_mul(
                Wext().rearrange("p (a t) -> p a t", t=128),
                psA[0:64, 0:2048].rearrange("p (a t) -> p a t", t=128),
                cvh()[0:64, 17 * 256 + qh:17 * 256 + qh + 128].unsqueeze(1)
                .broadcast_to((64, NH, 128))),
                r=tuple(pqA) + (CVB, B("Wext")), w=(B("Wext"),), tag=f"B1Wx{c}_{q}")

            for h in range(NH):
                sc.op("pe", lambda e, h=h, qh=qh: e.transpose(
                    psBb()[0:128, (h % 4) * 1024:(h % 4) * 1024 + 128],
                    cvh()[:, h * 256 + qh:h * 256 + qh + 128], sb["idb"][:]),
                    r=(CVB, B("idb")), w=(pqB[h % 4],), tag=f"B1T{c}_{q}_{h}")
                sc.op("dve", lambda e, h=h: e.tensor_copy(
                    Xt()[:, h * 128:(h + 1) * 128],
                    psBb()[0:128, (h % 4) * 1024:(h % 4) * 1024 + 128]),
                    r=(pqB[h % 4], B("XtB")), w=(B("XtB"),), tag=f"B1Te{c}_{q}_{h}")
            sg = SGEN[0]
            for h in range(NH):
                sc.op("pe", lambda e, h=h: e.matmul(
                    psA[:, h * 128:(h + 1) * 128], Xt()[:, h * 128:(h + 1) * 128],
                    Wpp()[:, h * 128:(h + 1) * 128], start=True, stop=False),
                    r=(B("XtB"), B("Wpp")), w=(pqA[h // 4],), tag=f"B1y0{c}_{q}_{h}")
                sc.op("pe", lambda e, h=h: e.matmul(
                    psA[:, h * 128:(h + 1) * 128], Xt()[:, h * 128:(h + 1) * 128],
                    sb["DI"][:, h * 128:(h + 1) * 128], start=False, stop=False),
                    r=(B("XtB"), B("DI")), w=(pqA[h // 4],), tag=f"B1y1{c}_{q}_{h}")
                sc.op("pe", lambda e, h=h, sg=sg: e.matmul(
                    psA[:, h * 128:(h + 1) * 128], sview(sg, h),
                    Wext()[:, h * 128:(h + 1) * 128], start=False, stop=True),
                    r=(B(f"S{sg}"), B("Wext")), w=(pqA[h // 4],), tag=f"B1y2{c}_{q}_{h}")
            zgst = zg()
            sc.op("act", lambda e, q=q: e.activation(
                zsil().rearrange("p (a t) -> p a t", t=128),
                bass.AP(zgst.tensor, zgst.offset + q * Q,
                        [[zgst.ap[0][0], 128], [CH, NH], [1, Q]]), AF.Silu),
                r=tuple(ZGS) + (B("zsil"),), w=(B("zsil"),), tag=f"B1zs{c}_{q}")
            for h in range(NH):
                sc.op("dve", lambda e, h=h, q=q: e.scalar_tensor_tensor(
                    bass.AP(zgst.tensor, zgst.offset + h * CH + q * Q,
                            [[zgst.ap[0][0], 128], [1, Q]]),
                    psA[:, h * 128:(h + 1) * 128], sb["nwcol"][:, h:h + 1],
                    zsil()[:, h * 128:(h + 1) * 128], ALU.mult, ALU.mult),
                    r=(pqA[h // 4], B("nwcol"), B("zsil"), ZGS[h]),
                    w=(ZGS[h],), tag=f"B1g{c}_{q}_{h}")
            for h in range(NH):
                bd = B(f"bd{h % 2}")
                sc.op("dve", lambda e, h=h: e.tensor_scalar(
                    bdec()[:, (h % 2) * DS:(h % 2 + 1) * DS], btm(),
                    sb["dcol"][:, h:h + 1], None, ALU.mult),
                    r=(B("btm"), B("dcol"), bd), w=(bd,), tag=f"B1bd{c}_{q}_{h}")
                sc.op("pe", lambda e, h=h: e.matmul(
                    psB[0:64, h * 128:(h + 1) * 128], bdec()[:, (h % 2) * DS:(h % 2 + 1) * DS],
                    Xt()[:, h * 128:(h + 1) * 128], start=True, stop=True),
                    r=(bd, B("XtB")), w=(pqB[h // 4],), tag=f"B1u0{c}_{q}_{h}")
            for h in range(NH):
                sc.op("dve", lambda e, h=h, sg=sg, q=q: e.scalar_tensor_tensor(
                    sview(1 - sg, h), sview(sg, h), esq()[:, h * NSUB + q:h * NSUB + q + 1],
                    psB[0:64, h * 128:(h + 1) * 128], ALU.mult, ALU.add),
                    r=(B(f"S{sg}"), EQ, pqB[h // 4], B(f"S{1 - sg}")),
                    w=(B(f"S{1 - sg}"),), tag=f"B1u1{c}_{q}_{h}")
            SGEN[0] = 1 - sg

        for k in range(EIP):
            sq = B(f"B1r{k % 2}")
            sc.op("act", lambda e, k=k: e.activation(
                sb["sqpp"][:, (k % 2) * CH:(k % 2 + 1) * CH],
                zg()[:, k * CH:(k + 1) * CH], AF.Square),
                r=(ZGS[k], sq), w=(sq,), tag=f"B1s2{c}_{k}")
            sc.op("pe", lambda e, k=k: e.matmul(
                psA[0:1, 0:CH], sb["inw2"][:, k:k + 1],
                sb["sqpp"][:, (k % 2) * CH:(k % 2 + 1) * CH],
                start=(k == 0), stop=(k == EIP - 1)),
                r=(sq, B("inw2")), w=(pqA[0],), tag=f"B1sm{c}_{k}")
        rs = B("rowst")
        sc.op("dve", lambda e: e.tensor_copy(sb["rowst"][:], psA[0:1, 0:CH]),
              r=(pqA[0], rs), w=(rs,), tag=f"B1rc{c}")
        sc.op("pe", lambda e: e.matmul(psB[:, 0:CH], sb["ones1"][:], sb["rowst"][:],
                                       start=True, stop=True),
              r=(rs, B("ones1")), w=(pqB[0],), tag=f"B1rb{c}")
        sc.op("dve", lambda e: e.tensor_scalar(sb["bc0"][:], psB[:, 0:CH], 1.0 / EI, 1e-5,
                                               ALU.mult, ALU.add),
              r=(pqB[0], B("bc0")), w=(B("bc0"),), tag=f"B1r0{c}")
        sc.op("act", lambda e: e.activation(sb["bc1"][:], sb["bc0"][:], AF.Sqrt),
              r=(B("bc0"), B("bc1")), w=(B("bc1"),), tag=f"B1r1{c}")
        sc.op("dve", lambda e: e.reciprocal(sb["bc2"][:], sb["bc1"][:]),
              r=(B("bc1"), B("bc2")), w=(B("bc2"),), tag=f"B1r2{c}")
        for k in range(EIP):
            wk = B(f"B1w{k % 2}")
            dma("gp", wkwin()[:, (k % 2) * 1024:(k % 2 + 1) * 1024],
                d_wopt.ap()[k * 128:(k + 1) * 128, :], f"B1o{c}_{k}", r=(wk,), w=(wk,))
            for o in range(8):
                ps = psA if o < 4 else psB
                pq = pqA[o % 4] if o < 4 else pqB[o % 4]
                sc.op("pe", lambda e, ps=ps, o=o, k=k: e.matmul(
                    ps[:, (o % 4) * CH:(o % 4 + 1) * CH],
                    wkwin()[:, (k % 2) * 1024 + o * 128:(k % 2) * 1024 + o * 128 + 128],
                    zg()[:, k * CH:(k + 1) * CH], start=(k == 0), stop=(k == EIP - 1)),
                    r=(wk, ZGS[k]), w=(pq,), tag=f"B1p{c}_{o}_{k}")
        for o in range(8):
            ps = psA if o < 4 else psB
            pq = pqA[o % 4] if o < 4 else pqB[o % 4]
            xsv = xslt[0]
            xsb = (B("Bxs0"),)
            dma("sp", xsv(), d_x1.ap()[o * 128:(o + 1) * 128, c * CH:(c + 1) * CH],
                f"B1x2{c}_{o}", r=xsb, w=xsb)
            rmk = "rm0" if o % 2 == 0 else "rm1"
            ot = "bc3" if o % 2 == 0 else "bc1"
            sc.op("dve", lambda e, ps=ps, o=o, rmk=rmk: e.tensor_mul(
                sb[rmk][:], ps[:, (o % 4) * CH:(o % 4 + 1) * CH], sb["bc2"][:]),
                r=(pq, B("bc2"), B(rmk)), w=(B(rmk),), tag=f"B1e0{c}_{o}")
            sc.op("dve", lambda e, o=o, rmk=rmk, ot=ot, xsv=xsv: e.scalar_tensor_tensor(
                sb[ot][:], sb[rmk][:], 0.0, xsv(), ALU.bypass, ALU.add),
                r=(B(rmk),) + xsb + (B(ot),), w=(B(ot),), tag=f"B1e1{c}_{o}")
            dma("act", d_x2.ap()[o * 128:(o + 1) * 128, c * CH:(c + 1) * CH],
                sb[ot][:], f"B1st{c}_{o}", r=(B(ot),), w=(B(f"B1d{c}_{o}"),))

    for c in range(NCH):
        mamba_chunk(c)
    sc.barrier()
    if phases == "AB":
        dma("sp", d_out.ap(), d_x2.ap(), "cpout", r=(), w=())
        sc.finalize()
        _emit(nc, sc, sems, FILLSCR)
        return nc

    for j in range(4):
        dma("sp", wbv(j * 16384, 128, DP * HF // 4),
            d_w1q.ap()[:, j * (DP * HF // 4):(j + 1) * (DP * HF // 4)],
            f"w1r{j}", r=(W1S[j],), w=(W1S[j],))
    ffn_phase("C", d_x2, d_out)

    sc.finalize()
    _emit(nc, sc, sems, FILLSCR)
    return nc


def _emit(nc, sc, sems, fillscr=None):
    fills = {}
    if fillscr is not None:
        fills = {
            "dve": lambda e: e.tensor_copy(fillscr[0:1, 0:1], fillscr[0:1, 1:2]),
            "act": lambda e: e.activation(fillscr[0:1, 2:3], fillscr[0:1, 3:4], AF.Copy),
        }
    with nc.Block() as block:
        def run(eng_name):
            def body(eng):
                sc.emit(eng_name, eng, sems, fills.get(eng_name))
                if eng_name == "sp":
                    for qn, tot in sc.dma_totals.items():
                        eng.wait_ge(sems["d" + qn], tot)
            return body
        block.sync(run("sp"))
        block.gpsimd(run("gp"))
        block.tensor(run("pe"))
        block.vector(run("dve"))
        block.scalar(run("act"))


_NC_CACHE = [None]


def kernel(**inputs):
    f32 = np.float32

    def cc(a):
        return np.ascontiguousarray(np.asarray(a), dtype=f32)

    u = np.asarray(inputs["L0_attn_u"])
    ip = np.asarray(inputs["L0_attn_in_proj_w"])
    base = dict(
        w1t=cc(np.asarray(inputs["L0_ffn_w1"]).T),
        w2t=cc(np.asarray(inputs["L0_ffn_w2"]).T),
        wipt=cc(ip[:ZXB].T),
        wiptdt=cc(ip[ZXB:].T),
        wopt=cc(np.asarray(inputs["L0_attn_out_proj_w"]).T),
        fcw=cc(np.asarray(inputs["L0_ffn_conv_w"]).reshape(HF, 3)),
        fcb=cc(np.asarray(inputs["L0_ffn_conv_b"]).reshape(HF, 1)),
        acw=cc(np.concatenate([np.asarray(inputs["L0_attn_conv_w"]).reshape(CONVD, 4),
                               np.asarray(inputs["L0_attn_conv_w"]).reshape(CONVD, 4)[2112:2176],
                               np.zeros((64, 4), np.float32)])),
        acb=cc(np.concatenate([np.asarray(inputs["L0_attn_conv_b"]).reshape(CONVD, 1),
                               np.asarray(inputs["L0_attn_conv_b"]).reshape(CONVD, 1)[2112:2176],
                               np.zeros((64, 1), np.float32)])),
        dtb=cc(np.asarray(inputs["L0_attn_dt_bias"]).reshape(NH, 1)),
        alog=cc(np.asarray(inputs["L0_attn_A_log"]).reshape(NH, 1)),
        drow=cc(np.repeat(np.asarray(inputs["L0_attn_D_param"]), 128).reshape(1, EI)),
        nwc=cc(np.asarray(inputs["L0_attn_norm_w"]).reshape(1, EI)),
        ident=np.eye(128, dtype=f32),
        mnegT=np.where(np.arange(128)[:, None] >= np.arange(128)[None, :],
                       np.float32(0.0), np.float32(NEG)).astype(f32),
        indic=np.kron(np.eye(NH, dtype=f32), np.ones((1, 128), dtype=f32)),
        onesr=np.ones((1, T), dtype=f32),
    )
    in_maps = [dict(base, u=cc(u[i].T)) for i in range(8)]
    if _NC_CACHE[0] is None:
        _NC_CACHE[0] = build()
    res = run_bass_kernel_spmd(_NC_CACHE[0], in_maps, core_ids=list(range(8)))
    out = np.stack([np.asarray(r["out"]).T for r in res.results], axis=0)
    return out.astype(np.float32)


# version 08 (final)
# speedup vs baseline: 1.0138x; 1.0138x over previous
import numpy as np
import ml_dtypes
import concourse.bass as bass
import concourse.mybir as mybir
from concourse.bass_utils import run_bass_kernel_spmd
from concourse.bass_isa import ReduceOp
from concourse.library_config import attn as _attn_lib

F32 = mybir.dt.float32
BF16 = mybir.dt.bfloat16
AF = mybir.ActivationFunctionType
ALU = mybir.AluOpType

MAGIC = 3 * 2.0 ** 22
NEG = -1e9

T, D, HF, EI, NH, DS = 2048, 1024, 4096, 2048, 16, 64
DP, HP, EIP = 8, 32, 16
CONVD, CVP = 2176, 17
CVP2 = 18
ZXB, ZXP = 4224, 33
CH, NCH, Q, NSUB = 512, 4, 128, 4
HSL = CH + 2
XSL = CH + 3


class Buf:
    __slots__ = ("name", "writer", "readers")

    def __init__(self, name):
        self.name, self.writer, self.readers = name, None, []


class Op:
    __slots__ = ("eng", "fn", "idx", "dma", "deps", "waits", "inc", "tag", "need",
                 "rawsrc", "engpos", "fill")

    def __init__(self, eng, fn, idx, dma, tag):
        self.eng, self.fn, self.idx, self.dma, self.tag = eng, fn, idx, dma, tag
        self.deps, self.waits, self.inc, self.need = [], {}, None, False
        self.rawsrc, self.engpos, self.fill = [], -1, False


class Sched:
    ENGS = ("sp", "gp", "pe", "dve", "act")

    def __init__(self):
        self.ops = []
        self.bufs = {}

    def B(self, name):
        b = self.bufs.get(name)
        if b is None:
            b = self.bufs[name] = Buf(name)
        return b

    def op(self, eng, fn, r=(), w=(), dma=None, tag=""):
        o = Op(eng, fn, len(self.ops), dma, tag)
        deps = {}
        for b in r:
            wtr = b.writer
            if wtr is not None and wtr.dma is None and wtr.eng == eng:
                o.rawsrc.append(wtr)
        for b in r:
            if b.writer is not None:
                deps[b.writer.idx] = b.writer
        for b in w:
            if b.writer is not None:
                deps[b.writer.idx] = b.writer
            for rd in b.readers:
                deps[rd.idx] = rd
        for b in r:
            b.readers.append(o)
        for b in w:
            b.writer, b.readers = o, []
        for d in deps.values():
            if d.dma is None and d.eng == eng:
                continue
            d.need = True
            o.deps.append(d)
        self.ops.append(o)
        return o

    def barrier(self):
        last = {}
        for o in self.ops:
            last[(o.eng, o.dma is not None)] = o
        for o in last.values():
            o.need = True
        for e in self.ENGS:
            deps = [o for o in last.values() if not (o.dma is None and o.eng == e)]
            bo = self.op(e, lambda eng: eng.nop(), tag="barrier")
            bo.deps.extend(deps)

    def finalize(self):
        pos = {}
        for o in self.ops:
            o.engpos = pos.get(o.eng, 0)
            pos[o.eng] = o.engpos + 1
        self.nfill = 0
        for o in self.ops:
            if any(w.engpos == o.engpos - 1 for w in o.rawsrc):
                o.fill = True
                self.nfill += 1
        ecount, dcount = {}, {}
        for o in self.ops:
            if o.dma is not None:
                dcount[o.dma] = dcount.get(o.dma, 0) + 16
                o.inc = ("d" + o.dma, dcount[o.dma])
            elif o.need:
                ecount[o.eng] = ecount.get(o.eng, 0) + 1
                o.inc = ("e" + o.eng, ecount[o.eng])
        for o in self.ops:
            wmap = {}
            for d in o.deps:
                assert d.idx < o.idx
                s, v = d.inc
                if wmap.get(s, -1) < v:
                    wmap[s] = v
            o.waits = wmap
        self.dma_totals = dcount

    def emit(self, eng_name, eng, sems, fill_fn=None):
        seen = {}
        for o in self.ops:
            if o.eng != eng_name:
                continue
            for s in sorted(o.waits):
                v = o.waits[s]
                if seen.get(s, -1) >= v:
                    continue
                seen[s] = v
                eng.wait_ge(sems[s], v)
            if o.fill and fill_fn is not None:
                fill_fn(eng)
            ins = o.fn(eng)
            if o.inc is not None:
                if isinstance(ins, (list, tuple)):
                    ins = ins[-1]
                assert ins is not None, o.tag
                ins.then_inc(sems[o.inc[0]], 16 if o.dma is not None else 1)


def build(phases="ABC", dump=None):
    nc = bass.Bass("TRN2", target_bir_lowering=False)
    sc = Sched()
    B = sc.B

    def din(name, shape):
        return nc.dram_tensor(name, list(shape), F32, kind="ExternalInput")

    d_u = din("u", (D, T))
    d_w1t = din("w1t", (D, HF))
    d_w2t = din("w2t", (HF, D))
    d_wipt = din("wipt", (D, ZXB))
    d_wiptdt = din("wiptdt", (D, NH))
    d_wopt = din("wopt", (EI, D))
    d_fcw = din("fcw", (HF, 3))
    d_fcb = din("fcb", (HF, 1))
    d_acw = din("acw", (CONVD + 128, 4))
    d_acb = din("acb", (CONVD + 128, 1))
    d_dtb = din("dtb", (NH, 1))
    d_alog = din("alog", (NH, 1))
    d_drow = din("drow", (1, EI))
    d_nwc = din("nwc", (1, EI))
    d_ident = din("ident", (128, 128))
    d_mnegT = din("mnegT", (128, 128))
    d_indic = din("indic", (NH, NH * 128))
    d_onesr = din("onesr", (1, T))

    d_out = nc.dram_tensor("out", [D, T], F32, kind="ExternalOutput")
    d_dbg = nc.dram_tensor("dbg", [128, 16384], F32, kind="ExternalOutput")
    d_x1 = nc.dram_tensor("xbuf1", [D, T], F32, kind="Internal")
    d_x2 = nc.dram_tensor("xbuf2", [D, T], F32, kind="Internal")
    d_xln = nc.dram_tensor("xlnb", [D, T], BF16, kind="Internal")
    d_w1q = nc.dram_tensor("w1stash", [128, DP * HF], BF16, kind="Internal")
    d_w2q = nc.dram_tensor("w2stash", [128, HP * D], BF16, kind="Internal")

    sb = {}

    def S(name, shape, dt):
        sb[name] = nc.alloc_sbuf_tensor("s_" + name, list(shape), dt)
        return sb[name]

    S("idf", (128, 128), F32)
    S("idb", (128, 128), BF16)
    S("mnegTb", (128, 128), BF16)
    S("idrepb", (128, NH * 128), BF16)
    S("ones1", (1, 128), F32)
    S("onescb", (128, 1), BF16)
    S("onescf", (128, 1), F32)
    S("eps5", (128, 1), F32)
    S("fcw", (128, HP * 3), F32)
    S("fcb", (128, HP), F32)
    S("acw", (128, CVP2 * 4), F32)
    S("acb", (128, CVP2), F32)
    S("dcol", (128, NH), F32)
    S("nwcol", (128, EIP), F32)
    S("inw2", (128, EIP), BF16)
    S("dtb", (NH, 1), F32)
    S("alog", (NH, 1), F32)
    S("negA", (NH, 1), F32)
    S("wsc", (128, 4), F32)
    S("rowsc", (1, 8), F32)
    S("DI", (128, NH * 128), BF16)
    S("Msb", (128, 128), BF16)
    S("sqpp", (128, 2 * CH), BF16)
    S("tmpc", (128, 2 * CH), BF16)
    S("cqs", (128, 2 * CH), BF16)
    S("rm0", (128, CH), F32)
    S("rm1", (128, CH), F32)
    S("bc0", (128, CH), F32)
    S("bc1", (128, CH), F32)
    S("bc2", (128, CH), F32)
    S("bc3", (128, CH), F32)
    S("rowst", (1, CH), F32)

    WREG = nc.alloc_sbuf_tensor("WREG", [128, 17920], F32)
    BIG = nc.alloc_sbuf_tensor("BIGREG", [128, 27200], F32)
    FILLSCR = nc.alloc_sbuf_tensor("fillscr", [1, 4], F32)

    def fv(off_b, p, n):
        assert off_b % 4 == 0
        return BIG.ap()[0:p, off_b // 4: off_b // 4 + n]

    def bv(off_b, p, n):
        assert off_b % 2 == 0
        return BIG.bitcast(BF16).ap()[0:p, off_b // 2: off_b // 2 + n]

    def wfv(off_b, p, n):
        return WREG.ap()[0:p, off_b // 4: off_b // 4 + n]

    def wbv(off_b, p, n):
        return WREG.bitcast(BF16).ap()[0:p, off_b // 2: off_b // 2 + n]

    w1s = lambda: wbv(0, 128, DP * HF)
    wips = lambda: wbv(0, 128, DP * ZXB)
    wkwin = lambda: wbv(67584, 128, 2 * 1024)
    cvt = wbv(65536, 128, 2 * CH)

    hbuf = [lambda o=o: bv(o * 32896, 128, HP * HSL) for o in range(2)]
    xwin = [lambda o=o: fv(65792 + o * 16384, 128, DP * CH) for o in range(2)]
    xqb = lambda: bv(98560, 128, DP * CH)
    zg = lambda: bv(0, 128, EIP * CH)
    xbc = lambda: bv(16384, 128, CVP2 * XSL)
    cvh = lambda: bv(34944, 128, CVP2 * 256)
    Xt = lambda: bv(44160, 128, NH * 128)
    Wpp = lambda: bv(48256, 128, NH * 128)
    Wext = lambda: bv(52352, 64, NH * 128)
    zsil = lambda: bv(56448, 128, NH * 128)
    Esb = lambda: bv(60544, 128, NH * 128)
    SREG = 64640
    xlnw = lambda: bv(72832, 128, DP * CH)
    xslt = [lambda o=o: fv(81024, 128, CH) for o in range(1)]
    srows = lambda: fv(83072, NH, T)
    packL = lambda: fv(91264, 17, T)
    packR = lambda: fv(99456, 17, T)
    bdec = lambda: bv(107648, 128, 2 * DS)
    btm = lambda: bv(107904, 128, DS)
    lex = lambda: fv(108032, 17, 64)
    esq = lambda: fv(108288, 64, NSUB * NH)
    wdt = lambda: bv(108544, 128, DP * NH)
    dtr = lambda: fv(0, NH, T)
    atr = lambda: fv(8192, NH, T)
    wstgA = lambda: fv(0, 128, 4096)
    wstgB = lambda: fv(16384, 128, 4096)
    wstgQ = lambda: bv(32768, 128, 4096)
    accw = lambda: fv(40960, 128, 32)
    absq = lambda: bv(49792, 128, 4096)
    drows = lambda: fv(41600, 1, EI)

    def sview(gen, h):
        return bv(SREG, 64, 2 * NH * 128)[0:64, gen * NH * 128 + h * 128:
                                          gen * NH * 128 + (h + 1) * 128]

    psA = nc.alloc_psum_tensor("psA", [128, 2048], F32)
    psBb = lambda: psB.bitcast(BF16).ap()
    psB = nc.alloc_psum_tensor("psB", [128, 2048], F32)
    pqA = [B("psA%d" % i) for i in range(4)]
    pqB = [B("psB%d" % i) for i in range(4)]

    sems = {}
    for e in Sched.ENGS:
        sems["e" + e] = nc.alloc_semaphore("e_" + e)
    for e in ("sp", "gp", "act"):
        sems["d" + e] = nc.alloc_semaphore("d_" + e)

    def dma(q, dst, src, tag, r=(), w=(), slow=False):
        sc.op(q, lambda e, dst=dst, src=src, slow=slow: e.dma_start(
            out=dst, in_=src, allow_slow_non_contiguous=slow),
            r=r, w=w, dma=q, tag=tag)

    def colap(apv, col, count, step):
        return bass.AP(apv.tensor, apv.offset + col, [[apv.ap[0][0], apv.ap[0][1]], [step, count]])


    def part_max(tagp, src_name):
        for b in range(4):
            sc.op("pe", lambda e, b=b: e.transpose(
                psB[:, b * 128:(b + 1) * 128],
                sb[src_name][:, b * 128:(b + 1) * 128], sb["idf"][:]),
                r=(B(src_name), B("idf")), w=(pqB[0],), tag=f"{tagp}tp{b}")
        sc.op("dve", lambda e: e.tensor_reduce(
            sb["rm0"][:, 0:4], psB[:, 0:512].rearrange("j (b pp) -> j b pp", pp=128),
            mybir.AxisListType.X, ALU.max),
            r=(pqB[0], B("rm0")), w=(B("rm0"),), tag=f"{tagp}rd")
        for b in range(4):
            sc.op("gp", lambda e, b=b: e.dma_start(
                out=sb["rowst"][0:1, b * 128:(b + 1) * 128],
                in_=sb["rm0"][:, b:b + 1], allow_slow_non_contiguous=True),
                r=(B("rm0"), B("rowst")), w=(B("rowst"),), dma="gp", tag=f"{tagp}fl{b}")
        sc.op("pe", lambda e: e.matmul(psB[:, 512:1024], sb["ones1"][:], sb["rowst"][:],
                                       start=True, stop=True),
            r=(B("rowst"), B("ones1")), w=(pqB[1],), tag=f"{tagp}bc")

    def dump_hook(site, c, ap, nfree, rb=(), q="act"):
        if dump != site or (nfree > 4096 and c > 0):
            return
        dma(q, d_dbg.ap()[:, c * 4096: c * 4096 + nfree], ap,
            f"dump{site}{c}", r=rb, w=(B("dbgw"),))

    dma("sp", sb["idf"][:], d_ident.ap(), "idf", w=(B("idf"),))
    dma("gp", sb["idb"][:], d_ident.ap(), "idb", w=(B("idb"),))
    dma("gp", sb["mnegTb"][:], d_mnegT.ap(), "mnegT", w=(B("mnegTb"),))
    dma("sp", sb["fcw"][:].rearrange("p (a k) -> p a k", k=3),
        d_fcw.ap().rearrange("(a p) k -> p a k", p=128), "fcw", w=(B("fcw"),))
    dma("sp", sb["fcb"][:].rearrange("p (a k) -> p a k", k=1),
        d_fcb.ap().rearrange("(a p) k -> p a k", p=128), "fcb", w=(B("fcb"),), slow=True)
    dma("sp", sb["acw"][:].rearrange("p (a k) -> p a k", k=4),
        d_acw.ap().rearrange("(a p) k -> p a k", p=128), "acw", w=(B("acw"),))
    dma("sp", sb["acb"][:].rearrange("p (a k) -> p a k", k=1),
        d_acb.ap().rearrange("(a p) k -> p a k", p=128), "acb", w=(B("acb"),), slow=True)
    dma("sp", sb["nwcol"][:].rearrange("p (o a) -> p o a", o=1),
        d_nwc.ap().rearrange("o (a p) -> p o a", p=128), "nwc", w=(B("nwcol"),), slow=True)
    dma("sp", sb["dtb"][:], d_dtb.ap(), "dtb", w=(B("dtb"),))
    dma("sp", sb["alog"][:], d_alog.ap(), "alog", w=(B("alog"),))
    dma("sp", packR()[0:16, :], d_indic.ap(), "indic", w=(B("packind"),))
    dma("gp", wdt(), d_wiptdt.ap().rearrange("(k p) h -> p k h", p=128), "wdt", w=(B("wdt"),))
    dma("sp", drows(), d_drow.ap(), "drow", w=(B("drows"),))

    sc.op("dve", lambda e: e.memset(sb["ones1"][:], 1.0), w=(B("ones1"),), tag="o1")
    sc.op("dve", lambda e: e.memset(sb["onescb"][:], 1.0), w=(B("onescb"),), tag="ocb")
    sc.op("dve", lambda e: e.memset(sb["onescf"][:], 1.0), w=(B("onescf"),), tag="ocf")
    sc.op("dve", lambda e: e.memset(sb["eps5"][:], 1e-5), w=(B("eps5"),), tag="eps")
    dma("sp", packL()[16:17, :], d_onesr.ap(), "pk0", w=(B("packd0"),))
    dma("sp", lex()[16:17, :], d_onesr.ap()[0:1, 0:64], "lx0", w=(B("lex0"),))
    for i in range(NH):
        sc.op("dve", lambda e, i=i: e.tensor_copy(sb["idrepb"][:, i * 128:(i + 1) * 128], sb["idb"][:]),
              r=(B("idb"),), w=(B("idrepb"),), tag=f"irep{i}")

    sc.op("act", lambda e: e.activation(sb["negA"][:], sb["alog"][:], AF.Exp),
          r=(B("alog"),), w=(B("negA"),), tag="nA0")
    sc.op("dve", lambda e: e.tensor_scalar(sb["negA"][:], sb["negA"][:], -1.0, None, ALU.mult),
          r=(B("negA"),), w=(B("negA"),), tag="nA1")
    sc.op("dve", lambda e: e.tensor_mul(sb["bc0"][:, 0:EIP], sb["nwcol"][:], sb["nwcol"][:]),
          r=(B("nwcol"),), w=(B("bc0"),), tag="iw0")
    sc.op("dve", lambda e: e.reciprocal(sb["bc1"][:, 0:EIP], sb["bc0"][:, 0:EIP]),
          r=(B("bc0"),), w=(B("bc1"),), tag="iw1")
    sc.op("dve", lambda e: e.tensor_copy(sb["inw2"][:], sb["bc1"][:, 0:EIP]),
          r=(B("bc1"),), w=(B("inw2"),), tag="iw2")

    def dibc(e):
        ins = None
        for j in range(4):
            ins = e.matmul(psB[:, j * 512:(j + 1) * 512], sb["ones1"][:],
                           drows()[0:1, j * 512:(j + 1) * 512], start=True, stop=True)
        return ins
    sc.op("pe", dibc, r=(B("ones1"), B("drows")), w=tuple(pqB), tag="DIbc")
    sc.op("dve", lambda e: e.tensor_mul(
        sb["DI"][:].rearrange("p (a q) -> p a q", q=128),
        sb["idb"][:].unsqueeze(1).broadcast_to((128, NH, 128)),
        psB[:, 0:NH * 128].rearrange("p (a q) -> p a q", q=128)),
        r=tuple(pqB) + (B("idb"),), w=(B("DI"),), tag="DI")

    def prep_quant(wt_dram, kp, mfree, stash, invcol, wscol, tag, mslot, islot,
                   qn, pb, offA, offB, offQ, offacc, offabs, cpcol):
        stgA = lambda: fv(offA, 128, mfree)
        stgB = lambda: fv(offB, 128, mfree)
        stgQ = lambda: bv(offQ, 128, mfree)
        accw2 = lambda: fv(offacc, 128, kp)
        pq0, pq1 = pqB[pb // 512], pqB[pb // 512 + 1]
        stgev = [B(f"{tag}stg0"), B(f"{tag}stg1")]
        acc = B(f"{tag}acc")
        for k in range(kp):
            sg, stg = stgev[k % 2], (stgA if k % 2 == 0 else stgB)
            dma(qn, stg()[:, 0:mfree], wt_dram.ap()[k * 128:(k + 1) * 128, :],
                f"{tag}l{k}", r=(sg,), w=(sg,))
            sc.op("act", lambda e, stg=stg, k=k, mfree=mfree: e.activation(
                bv(offabs, 128, mfree)[:, 0:mfree], stg()[:, 0:mfree], AF.Abs,
                accum_out=accw2()[:, k:k + 1]),
                r=(sg, B(f"{tag}abs")), w=(B(f"{tag}abs"), acc), tag=f"{tag}a{k}")
        sc.op("pe", lambda e, kp=kp: e.matmul(psB[0:kp, pb:pb + 1], accw2()[:, 0:kp],
                                              sb["onescf"][:], start=True, stop=True),
              r=(acc, B("onescf")), w=(pq0,), tag=f"{tag}sA")
        sc.op("dve", lambda e, kp=kp: e.tensor_copy(
            sb["dcol"][0:kp, cpcol:cpcol + 1], psB[0:kp, pb:pb + 1]),
            r=(pq0,), w=(B(f"{tag}c1"),), tag=f"{tag}cp")
        sc.op("pe", lambda e, kp=kp: e.matmul(
            psB[0:1, pb + 512:pb + 513], sb["dcol"][0:kp, cpcol:cpcol + 1],
            sb["onescf"][0:kp, 0:1], start=True, stop=True),
            r=(B(f"{tag}c1"), B("onescf")), w=(pq1,), tag=f"{tag}sB")
        cnt = float(kp * 128 * mfree)
        sc.op("dve", lambda e: e.tensor_scalar(
            sb["rowsc"][0:1, mslot:mslot + 1], psB[0:1, pb + 512:pb + 513], 1.0 / cnt, 1e-5,
            ALU.mult, ALU.max), r=(pq1,), w=(B(f"{tag}m"),), tag=f"{tag}m")
        sc.op("dve", lambda e: e.reciprocal(
            sb["rowsc"][0:1, islot:islot + 1], sb["rowsc"][0:1, mslot:mslot + 1]),
            r=(B(f"{tag}m"),), w=(B(f"{tag}i"),), tag=f"{tag}i")
        sc.op("pe", lambda e: e.matmul(psB[:, pb + 512:pb + 513], sb["ones1"][:],
                                       sb["rowsc"][0:1, islot:islot + 1],
                                       start=True, stop=True),
              r=(B(f"{tag}i"), B("ones1")), w=(pq1,), tag=f"{tag}bi")
        sc.op("dve", lambda e: e.tensor_copy(sb["wsc"][:, invcol:invcol + 1],
                                             psB[:, pb + 512:pb + 513]),
              r=(pq1,), w=(B(f"{tag}ic"),), tag=f"{tag}icc")
        sc.op("pe", lambda e: e.matmul(psB[:, pb + 512:pb + 513], sb["ones1"][:],
                                       sb["rowsc"][0:1, mslot:mslot + 1],
                                       start=True, stop=True),
              r=(B(f"{tag}m"), B("ones1"), B(f"{tag}ic")), w=(pq1,), tag=f"{tag}bm")
        sc.op("dve", lambda e: e.tensor_copy(sb["wsc"][:, wscol:wscol + 1],
                                             psB[:, pb + 512:pb + 513]),
              r=(pq1,), w=(B(f"{tag}wsc"),), tag=f"{tag}wc")
        for k in range(kp):
            sg, stg = stgev[k % 2], (stgA if k % 2 == 0 else stgB)
            sg2, stg2 = stgev[(k + 1) % 2], (stgB if k % 2 == 0 else stgA)
            dma(qn, stg()[:, 0:mfree], wt_dram.ap()[k * 128:(k + 1) * 128, :],
                f"{tag}l2{k}", r=(sg,), w=(sg,))
            sc.op("dve", lambda e, stg=stg, stg2=stg2, mfree=mfree, invcol=invcol: e.tensor_scalar(
                stg2()[:, 0:mfree], stg()[:, 0:mfree], sb["wsc"][:, invcol:invcol + 1],
                MAGIC, ALU.mult, ALU.add),
                r=(sg, B(f"{tag}ic"), sg2), w=(sg2,), tag=f"{tag}q0{k}")
            sc.op("dve", lambda e, stg=stg, stg2=stg2, mfree=mfree: e.tensor_scalar(
                stg()[:, 0:mfree], stg2()[:, 0:mfree], MAGIC, 1.0, ALU.subtract, ALU.min),
                r=(sg2, sg), w=(sg,), tag=f"{tag}q1{k}")
            sc.op("dve", lambda e, stg=stg, stgQ=stgQ, mfree=mfree: e.tensor_scalar(
                stgQ()[:, 0:mfree], stg()[:, 0:mfree], -1.0, None, ALU.max),
                r=(sg, B(f"{tag}qq")), w=(B(f"{tag}qq"),), tag=f"{tag}q2{k}")
            dma("act", stash.ap()[:, k * mfree:(k + 1) * mfree], stgQ()[:, 0:mfree],
                f"{tag}st{k}", r=(B(f"{tag}qq"),), w=(B(f"{tag}sh{k}"),))

    prep_quant(d_w1t, DP, HF, d_w1q, 0, 1, "w1", 0, 1,
               "sp", 0, 0, 16384, 32768, 40960, 49792, 0)
    prep_quant(d_w2t, HP, D, d_w2q, 2, 3, "w2", 2, 3,
               "gp", 1024, 82176, 86272, 90368, 92416, 92544, 1)
    W1SHD = tuple(B(f"w1sh{k}") for k in range(DP))
    W2SHL = B(f"w2sh{HP - 1}")

    W1S = tuple(B("w1s%d" % j) for j in range(4))
    for j in range(4):
        dma("sp", wbv(j * 16384, 128, DP * HF // 4),
            d_w1q.ap()[:, j * (DP * HF // 4):(j + 1) * (DP * HF // 4)],
            f"w1l{j}", r=(W1S[j],) + W1SHD, w=(W1S[j],))

    def ffn_phase(ph, src, dst):
        HB = lambda g, o: B(f"{ph}h{g % 2}_{o}")

        def ln_stats(xbuf_ap, xbufB, c):
            for k in range(DP):
                sq = B(f"{ph}sq{k % 2}")
                sc.op("act", lambda e, k=k: e.activation(
                    sb["sqpp"][:, (k % 2) * CH:(k % 2 + 1) * CH],
                    xbuf_ap()[:, k * CH:(k + 1) * CH], AF.Square),
                    r=(xbufB, sq), w=(sq,), tag=f"{ph}sq{c}_{k}")
                sc.op("pe", lambda e, k=k: e.matmul(
                    psA[0:1, 0:CH], sb["onescb"][:], xbuf_ap()[:, k * CH:(k + 1) * CH],
                    start=(k == 0), stop=(k == DP - 1)),
                    r=(xbufB, B("onescb")), w=(pqA[0],), tag=f"{ph}mm{c}_{k}")
                sc.op("pe", lambda e, k=k: e.matmul(
                    psA[0:1, CH:2 * CH], sb["onescb"][:],
                    sb["sqpp"][:, (k % 2) * CH:(k % 2 + 1) * CH],
                    start=(k == 0), stop=(k == DP - 1)),
                    r=(sq, B("onescb")), w=(pqA[1],), tag=f"{ph}ms{c}_{k}")
            rs = B("rowst")
            sc.op("dve", lambda e: e.tensor_copy(sb["rowst"][:], psA[0:1, 0:CH]),
                  r=(pqA[0],), w=(rs,), tag=f"{ph}cm{c}")
            sc.op("pe", lambda e: e.matmul(psB[:, 0:CH], sb["ones1"][:], sb["rowst"][:],
                                           start=True, stop=True),
                  r=(rs, B("ones1")), w=(pqB[0],), tag=f"{ph}bm{c}")
            sc.op("dve", lambda e: e.tensor_scalar(sb["bc0"][:], psB[:, 0:CH], 1.0 / D, None, ALU.mult),
                  r=(pqB[0], B("bc0")), w=(B("bc0"),), tag=f"{ph}mb{c}")
            sc.op("dve", lambda e: e.tensor_copy(sb["rowst"][:], psA[0:1, CH:2 * CH]),
                  r=(pqA[1], rs), w=(rs,), tag=f"{ph}cs{c}")
            sc.op("pe", lambda e: e.matmul(psB[:, 0:CH], sb["ones1"][:], sb["rowst"][:],
                                           start=True, stop=True),
                  r=(rs, B("ones1"), B("bc0")), w=(pqB[0],), tag=f"{ph}bs{c}")
            sc.op("act", lambda e: e.activation(sb["bc2"][:], sb["bc0"][:], AF.Square),
                  r=(B("bc0"), B("bc2")), w=(B("bc2"),), tag=f"{ph}m2{c}")
            sc.op("dve", lambda e: e.scalar_tensor_tensor(
                sb["bc1"][:], psB[:, 0:CH], 1.0 / D, sb["bc2"][:], ALU.mult, ALU.subtract),
                r=(pqB[0], B("bc2"), B("bc1")), w=(B("bc1"),), tag=f"{ph}v{c}")
            sc.op("act", lambda e: e.activation(sb["bc2"][:], sb["bc1"][:], AF.Sqrt, bias=sb["eps5"][:, 0:1]),
                  r=(B("bc1"), B("bc2"), B("eps5")), w=(B("bc2"),), tag=f"{ph}sd{c}")
            sc.op("dve", lambda e: e.reciprocal(sb["bc1"][:], sb["bc2"][:]),
                  r=(B("bc2"), B("bc1")), w=(B("bc1"),), tag=f"{ph}r{c}")

        def stage_a(c):
            g = c % 2
            xw = B(f"{ph}xw{g}")
            xdep = (W2SHL,) if (ph == "A" and c == 1) else ()
            dma("sp", xwin[g]().rearrange("p (a t) -> p a t", a=DP),
                src.ap().rearrange("(a p) t -> p a t", p=128)[:, :, c * CH:(c + 1) * CH],
                f"{ph}x{c}", r=(xw,) + xdep, w=(xw,))
            xq = B(f"{ph}xq")
            for k in range(DP):
                sc.op("dve", lambda e, k=k, g=g: e.tensor_copy(
                    xqb()[:, k * CH:(k + 1) * CH], xwin[g]()[:, k * CH:(k + 1) * CH]),
                    r=(xw, xq), w=(xq,), tag=f"{ph}xb{c}_{k}")
            ln_stats(xqb, xq, f"A{c}")
            for k in range(DP):
                sc.op("dve", lambda e, k=k, g=g: e.tensor_sub(
                    sb["bc2"][:], xwin[g]()[:, k * CH:(k + 1) * CH], sb["bc0"][:]),
                    r=(xw, B("bc0"), B("bc2")), w=(B("bc2"),), tag=f"{ph}t{c}_{k}")
                sc.op("act", lambda e: e.activation(sb["bc3"][:], sb["bc2"][:], AF.Abs),
                      r=(B("bc2"), B("bc3")), w=(B("bc3"),), tag=f"{ph}ab{c}_{k}")
                if k == 0:
                    sc.op("dve", lambda e: e.tensor_copy(sb["rm0"][:], sb["bc3"][:]),
                          r=(B("bc3"), B("rm0")), w=(B("rm0"),), tag=f"{ph}rm{c}_0")
                else:
                    a, b_ = ("rm1", "rm0") if k % 2 else ("rm0", "rm1")
                    sc.op("dve", lambda e, a=a, b_=b_: e.tensor_tensor(
                        sb[a][:], sb[b_][:], sb["bc3"][:], ALU.max),
                        r=(B(b_), B("bc3"), B(a)), w=(B(a),), tag=f"{ph}rm{c}_{k}")
            part_max(f"{ph}pm{c}", "rm1")
            sc.op("dve", lambda e: e.tensor_mul(sb["bc2"][:], sb["bc1"][:], psB[:, 512:1024]),
                  r=(B("bc1"), pqB[1], B("bc2")), w=(B("bc2"),), tag=f"{ph}rx{c}")
            sc.op("dve", lambda e: e.tensor_scalar(sb["bc3"][:], sb["bc2"][:], 1e-5, None, ALU.max),
                  r=(B("bc2"), B("bc3")), w=(B("bc3"),), tag=f"{ph}mg{c}")
            sc.op("dve", lambda e: e.reciprocal(sb["bc2"][:], sb["bc3"][:]),
                  r=(B("bc3"), B("bc2")), w=(B("bc2"),), tag=f"{ph}ig{c}")
            sc.op("dve", lambda e: e.tensor_mul(sb["rm0"][:], sb["bc1"][:], sb["bc2"][:]),
                  r=(B("bc1"), B("bc2"), B("rm0")), w=(B("rm0"),), tag=f"{ph}a0{c}")
            sc.op("dve", lambda e: e.tensor_scalar(sb["bc1"][:], sb["rm0"][:], 127.0, None, ALU.mult),
                  r=(B("rm0"), B("bc1")), w=(B("bc1"),), tag=f"{ph}a1{c}")
            sc.op("dve", lambda e: e.tensor_scalar(sb["rm1"][:], sb["bc3"][:], 1.0 / 127, None, ALU.mult),
                  r=(B("bc3"), B("rm1")), w=(B("rm1"),), tag=f"{ph}be{c}")
            for k in range(DP):
                sc.op("dve", lambda e, k=k, g=g: e.tensor_sub(
                    sb["bc2"][:], xwin[g]()[:, k * CH:(k + 1) * CH], sb["bc0"][:]),
                    r=(xw, B("bc0"), B("bc2")), w=(B("bc2"),), tag=f"{ph}qs{c}_{k}")
                sc.op("dve", lambda e: e.tensor_mul(sb["rm0"][:], sb["bc2"][:], sb["bc1"][:]),
                      r=(B("bc2"), B("bc1"), B("rm0")), w=(B("rm0"),), tag=f"{ph}qm{c}_{k}")
                sc.op("dve", lambda e: e.tensor_scalar(
                    sb["bc2"][:], sb["rm0"][:], MAGIC, MAGIC, ALU.add, ALU.subtract),
                    r=(B("rm0"), B("bc2")), w=(B("bc2"),), tag=f"{ph}qr{c}_{k}")
                sc.op("dve", lambda e, k=k: e.tensor_mul(
                    xqb()[:, k * CH:(k + 1) * CH], sb["bc2"][:], sb["rm1"][:]),
                    r=(B("bc2"), B("rm1"), xq), w=(xq,), tag=f"{ph}qd{c}_{k}")
            if c == 0:
                dump_hook("wsc", c, sb["wsc"][:], 4, rb=(B("w1wsc"), B("w2wsc")), q="act")
                dump_hook("w1s", c, w1s()[:, 0:4096], 4096, rb=W1S, q="gp")
            dump_hook("xq", c, xqb()[:, 0:4096 if False else DP * CH][:, 0:4096], 4096,
                      rb=(xq,), q="gp")
            dump_hook("alpha", c, sb["bc1"][:], CH, rb=(B("bc1"),), q="act")
            dump_hook("beta", c, sb["rm1"][:], CH, rb=(B("rm1"),), q="act")
            dump_hook("mb", c, sb["bc0"][:], CH, rb=(B("bc0"),), q="act")
            dump_hook("rb", c, sb["bc1x" if False else "bc1"][:], CH, rb=(B("bc1"),), q="act")
            for grp in range(8):
                ps, pq = (psA, pqA) if grp % 2 == 0 else (psB, pqB)
                for oo in range(4):
                    o = grp * 4 + oo
                    for k in range(DP):
                        sc.op("pe", lambda e, ps=ps, oo=oo, o=o, k=k: e.matmul(
                            ps[:, oo * CH:(oo + 1) * CH],
                            w1s()[:, (k * HF + o * 128):(k * HF + o * 128 + 128)],
                            xqb()[:, k * CH:(k + 1) * CH],
                            start=(k == 0), stop=(k == DP - 1)),
                            r=(xq,) + W1S, w=(pq[oo],), tag=f"{ph}w1_{c}_{o}_{k}")
                for oo in range(4):
                    o = grp * 4 + oo
                    pq = (pqA if grp % 2 == 0 else pqB)[oo]
                    sc.op("act", lambda e, grp=grp, oo=oo, o=o, g=g: e.activation(
                        hbuf[g]()[:, o * HSL + 1: o * HSL + 1 + CH],
                        (psA if grp % 2 == 0 else psB)[:, oo * CH:(oo + 1) * CH],
                        AF.Copy, scale=sb["wsc"][:, 1:2]),
                        r=(pq, B("w1wsc"), HB(g, o)), w=(HB(g, o),), tag=f"{ph}he{c}_{o}")
            allh = tuple(HB(g, o) for o in range(HP))
            allhp = tuple(HB(1 - g, o) for o in range(HP))
            if c == 0:
                hzdep = (B("w1stg0"), B("w1stg1")) if ph == "A" else ()
                sc.op("dve", lambda e, g=g: e.memset(colap(hbuf[g](), 0, HP, HSL), 0.0),
                      r=allh + hzdep, w=allh, tag=f"{ph}hz{c}")
            else:
                sc.op("dve", lambda e, g=g: e.tensor_copy(
                    colap(hbuf[g](), 0, HP, HSL), colap(hbuf[1 - g](), CH, HP, HSL)),
                    r=allhp + allh, w=allh, tag=f"{ph}hl{c}")
                sc.op("dve", lambda e, g=g: e.tensor_copy(
                    colap(hbuf[1 - g](), CH + 1, HP, HSL), colap(hbuf[g](), 1, HP, HSL)),
                    r=allh + allhp, w=allhp, tag=f"{ph}hr{c}")
            if c == NCH - 1:
                sc.op("dve", lambda e, g=g: e.memset(colap(hbuf[g](), CH + 1, HP, HSL), 0.0),
                      r=allh, w=allh, tag=f"{ph}hz2{c}")

        def stage_b(p):
            g = p % 2
            xw = B(f"{ph}xw{g}")
            hs = hbuf[g]
            dump_hook("h", p, hs()[:, 0:8192], 8192,
                      rb=tuple(HB(g, o) for o in range(HP)), q="gp")
            for o2 in range(0, HP, 2):
                pair = [(o2, sb["tmpc"], B("tm0"), B("tm1")),
                        (o2 + 1, cvt, B("gt0"), B("gt1"))]
                for o, tb, t0, t1 in pair:
                    sc.op("dve", lambda e, base=o * HSL, o=o, tb=tb: e.tensor_scalar(
                        tb[:, 0:CH], hs()[:, base:base + CH],
                        sb["fcw"][:, 3 * o:3 * o + 1], None, ALU.mult),
                        r=(HB(g, o), B("fcw"), t0), w=(t0,), tag=f"{ph}c0{p}_{o}")
                for o, tb, t0, t1 in pair:
                    sc.op("dve", lambda e, base=o * HSL, o=o, tb=tb: e.scalar_tensor_tensor(
                        tb[:, CH:2 * CH], hs()[:, base + 1:base + 1 + CH],
                        sb["fcw"][:, 3 * o + 1:3 * o + 2], tb[:, 0:CH],
                        ALU.mult, ALU.add),
                        r=(HB(g, o), t0, B("fcw"), t1), w=(t1,), tag=f"{ph}c1{p}_{o}")
                for o, tb, t0, t1 in pair:
                    sc.op("dve", lambda e, base=o * HSL, o=o, tb=tb: e.scalar_tensor_tensor(
                        tb[:, 0:CH], hs()[:, base + 2:base + 2 + CH],
                        sb["fcw"][:, 3 * o + 2:3 * o + 3], tb[:, CH:2 * CH],
                        ALU.mult, ALU.add),
                        r=(HB(g, o), t1, B("fcw"), t0), w=(t0,), tag=f"{ph}c2{p}_{o}")
                for o, tb, t0, t1 in pair:
                    sc.op("act", lambda e, base=o * HSL, o=o, tb=tb: e.activation(
                        hs()[:, base + 1:base + 1 + CH], tb[:, 0:CH], AF.Silu,
                        bias=sb["fcb"][:, o:o + 1]),
                        r=(t0, B("fcb"), HB(g, o)), w=(HB(g, o),), tag=f"{ph}si{p}_{o}")
            for o in range(HP):
                base = o * HSL
                sc.op("act", lambda e, base=base: e.activation(
                    sb["bc3"][:], hs()[:, base + 1:base + 1 + CH], AF.Abs),
                    r=(HB(g, o), B("bc3")), w=(B("bc3"),), tag=f"{ph}cab{p}_{o}")
                if o == 0:
                    sc.op("dve", lambda e: e.tensor_copy(sb["rm0"][:], sb["bc3"][:]),
                          r=(B("bc3"), B("rm0")), w=(B("rm0"),), tag=f"{ph}cr{p}_0")
                else:
                    a, b_ = ("rm1", "rm0") if o % 2 else ("rm0", "rm1")
                    sc.op("dve", lambda e, a=a, b_=b_: e.tensor_tensor(
                        sb[a][:], sb[b_][:], sb["bc3"][:], ALU.max),
                        r=(B(b_), B("bc3"), B(a)), w=(B(a),), tag=f"{ph}cr{p}_{o}")
            part_max(f"{ph}cm{p}", "rm1")
            sc.op("dve", lambda e: e.tensor_scalar(sb["bc0"][:], psB[:, 512:1024], 1e-5, None, ALU.max),
                  r=(pqB[1], B("bc0")), w=(B("bc0"),), tag=f"{ph}cg{p}")
            sc.op("dve", lambda e: e.reciprocal(sb["bc2"][:], sb["bc0"][:]),
                  r=(B("bc0"), B("bc2")), w=(B("bc2"),), tag=f"{ph}ci{p}")
            sc.op("dve", lambda e: e.tensor_scalar(sb["bc1"][:], sb["bc2"][:], 127.0, None, ALU.mult),
                  r=(B("bc2"), B("bc1")), w=(B("bc1"),), tag=f"{ph}ca{p}")
            sc.op("dve", lambda e: e.tensor_scalar(sb["bc2"][:], sb["bc0"][:], 0.5 / 127.0,
                                                   sb["wsc"][:, 3:4], ALU.mult, ALU.mult),
                  r=(B("bc0"), B("w2wsc"), B("bc2")), w=(B("bc2"),), tag=f"{ph}cgm{p}")
            for k in range(HP):
                kb = B(f"{ph}cq{k % 2}")
                rmk = "rm0" if k % 2 == 0 else "rm1"
                sc.op("dve", lambda e, k=k, rmk=rmk: e.tensor_mul(
                    sb[rmk][:], hs()[:, k * HSL + 1:k * HSL + 1 + CH], sb["bc1"][:]),
                    r=(HB(g, k), B("bc1"), B(rmk)), w=(B(rmk),), tag=f"{ph}q0{p}_{k}")
                sc.op("dve", lambda e, k=k, rmk=rmk: e.tensor_scalar(
                    sb["cqs"][:, (k % 2) * CH:(k % 2 + 1) * CH], sb[rmk][:],
                    MAGIC, MAGIC, ALU.add, ALU.subtract),
                    r=(B(rmk), kb), w=(kb,), tag=f"{ph}q1{p}_{k}")
                wk = B(f"{ph}wk{k % 2}")
                wkdep = (W2SHL,) if (ph == "A" and p == 0) else ()
                dma("sp", wkwin()[:, (k % 2) * 1024:(k % 2 + 1) * 1024],
                    d_w2q.ap()[:, k * 1024:(k + 1) * 1024], f"{ph}w2l{p}_{k}",
                    r=(wk,) + wkdep, w=(wk,))
                for o in range(8):
                    ps = psA if o < 4 else psB
                    pq = pqA[o % 4] if o < 4 else pqB[o % 4]
                    sc.op("pe", lambda e, ps=ps, o=o, k=k: e.matmul(
                        ps[:, (o % 4) * CH:(o % 4 + 1) * CH],
                        wkwin()[:, (k % 2) * 1024 + o * 128:(k % 2) * 1024 + o * 128 + 128],
                        sb["cqs"][:, (k % 2) * CH:(k % 2 + 1) * CH],
                        start=(k == 0), stop=(k == HP - 1)),
                        r=(kb, wk), w=(pq,), tag=f"{ph}w2_{p}_{o}_{k}")
            dump_hook("c", p, hs()[:, 0:8192], 8192,
                      rb=tuple(HB(g, o) for o in range(HP)), q="gp")
            dump_hook("gam", p, sb["bc2"][:], CH, rb=(B("bc2"),), q="act")
            for o in range(8):
                ps = psA if o < 4 else psB
                pq = pqA[o % 4] if o < 4 else pqB[o % 4]
                rmk = "rm0" if o % 2 == 0 else "rm1"
                ot = "bc3" if o % 2 == 0 else "bc0"
                sc.op("dve", lambda e, ps=ps, o=o, rmk=rmk: e.tensor_mul(
                    sb[rmk][:], ps[:, (o % 4) * CH:(o % 4 + 1) * CH], sb["bc2"][:]),
                    r=(pq, B("bc2"), B(rmk)), w=(B(rmk),), tag=f"{ph}e0{p}_{o}")
                sc.op("dve", lambda e, o=o, g=g, rmk=rmk, ot=ot: e.scalar_tensor_tensor(
                    sb[ot][:], sb[rmk][:], 0.0,
                    xwin[g]()[:, o * CH:(o + 1) * CH], ALU.bypass, ALU.add),
                    r=(B(rmk), xw, B(ot)), w=(B(ot),), tag=f"{ph}e1{p}_{o}")
                dma("act", dst.ap()[o * 128:(o + 1) * 128, p * CH:(p + 1) * CH],
                    sb[ot][:], f"{ph}st{p}_{o}", r=(B(ot),), w=(B(f"{ph}d{p}_{o}"),))

        for c in range(NCH + 1):
            if c < NCH:
                stage_a(c)
            if c >= 1:
                stage_b(c - 1)

    ffn_phase("A", d_u, d_x1)
    sc.barrier()
    if phases == "A":
        dma("sp", d_out.ap(), d_x1.ap(), "cpout", r=(), w=())
        sc.finalize()
        _emit(nc, sc, sems, FILLSCR)
        return nc

    mkA = B("postA")
    sc.op("dve", lambda e: e.memset(sb["eps5"][0:1, 0:1], 1e-5), w=(mkA,), tag="postA")
    dma("sp", packR()[0:16, :], d_indic.ap(), "indicR", r=(mkA,), w=(B("packind"),))
    dma("sp", packL()[16:17, :], d_onesr.ap(), "pk0R", r=(mkA,), w=(B("packd0"),))
    WIPS = tuple(B("wips%d" % k) for k in range(DP))
    for k in range(DP):
        dma("gp", wbv(k * (ZXB * 2), 128, ZXB),
            d_wipt.ap()[k * 128:(k + 1) * 128, :], f"ipl{k}", r=(mkA,), w=(WIPS[k],))

    xlnw2 = lambda: bv(34944, 128, DP * CH)
    tmf = lambda: sb["tmpc"].bitcast(F32).ap()[0:128, 0:CH]
    TMB = (B("tm0"), B("tm1"))
    for c in range(NCH):
        xl = B("Bxln0")
        xlv = xlnw
        for k in range(DP):
            dma("gp", xlv()[:, k * CH:(k + 1) * CH],
                d_x1.ap()[k * 128:(k + 1) * 128, c * CH:(c + 1) * CH],
                f"B0b{c}_{k}", r=(xl, B(f"Ad{c}_{k}")), w=(xl,))
        for k in range(DP):
            sq = B(f"B0q{k % 2}")
            sc.op("act", lambda e, k=k, xlv=xlv: e.activation(
                sb["sqpp"][:, (k % 2) * CH:(k % 2 + 1) * CH],
                xlv()[:, k * CH:(k + 1) * CH], AF.Square),
                r=(xl, sq), w=(sq,), tag=f"B0s{c}_{k}")
            sc.op("pe", lambda e, k=k, xlv=xlv: e.matmul(
                psA[0:1, 0:CH], sb["onescb"][:], xlv()[:, k * CH:(k + 1) * CH],
                start=(k == 0), stop=(k == DP - 1)),
                r=(xl, B("onescb")), w=(pqA[0],), tag=f"B0m{c}_{k}")
            sc.op("pe", lambda e, k=k: e.matmul(
                psA[0:1, CH:2 * CH], sb["onescb"][:],
                sb["sqpp"][:, (k % 2) * CH:(k % 2 + 1) * CH],
                start=(k == 0), stop=(k == DP - 1)),
                r=(sq, B("onescb")), w=(pqA[1],), tag=f"B0ss{c}_{k}")
        rs = B("rowst")
        sc.op("dve", lambda e: e.tensor_copy(sb["rowst"][:], psA[0:1, 0:CH]),
              r=(pqA[0], rs), w=(rs,), tag=f"B0c{c}")
        sc.op("pe", lambda e: e.matmul(psB[:, 0:CH], sb["ones1"][:], sb["rowst"][:],
                                       start=True, stop=True),
              r=(rs, B("ones1")), w=(pqB[0],), tag=f"B0bm{c}")
        sc.op("dve", lambda e: e.tensor_scalar(sb["bc0"][:], psB[:, 0:CH], 1.0 / D, None, ALU.mult),
              r=(pqB[0], B("bc0")), w=(B("bc0"),), tag=f"B0mb{c}")
        sc.op("dve", lambda e: e.tensor_copy(sb["rowst"][:], psA[0:1, CH:2 * CH]),
              r=(pqA[1], rs), w=(rs,), tag=f"B0c2{c}")
        sc.op("pe", lambda e: e.matmul(psB[:, 0:CH], sb["ones1"][:], sb["rowst"][:],
                                       start=True, stop=True),
              r=(rs, B("ones1"), B("bc0")), w=(pqB[0],), tag=f"B0bs{c}")
        sc.op("act", lambda e: e.activation(sb["bc2"][:], sb["bc0"][:], AF.Square),
              r=(B("bc0"), B("bc2")), w=(B("bc2"),), tag=f"B0m2{c}")
        sc.op("dve", lambda e: e.scalar_tensor_tensor(
            sb["bc1"][:], psB[:, 0:CH], 1.0 / D, sb["bc2"][:], ALU.mult, ALU.subtract),
            r=(pqB[0], B("bc2"), B("bc1")), w=(B("bc1"),), tag=f"B0v{c}")
        sc.op("act", lambda e: e.activation(sb["bc2"][:], sb["bc1"][:], AF.Sqrt, bias=sb["eps5"][:, 0:1]),
              r=(B("bc1"), B("bc2"), B("eps5")), w=(B("bc2"),), tag=f"B0sd{c}")
        sc.op("dve", lambda e: e.reciprocal(sb["bc1"][:], sb["bc2"][:]),
              r=(B("bc2"), B("bc1")), w=(B("bc1"),), tag=f"B0r{c}")
        sc.op("dve", lambda e: e.tensor_mul(sb["bc2"][:], sb["bc0"][:], sb["bc1"][:]),
              r=(B("bc0"), B("bc1"), B("bc2")), w=(B("bc2"),), tag=f"B0mr{c}")
        for k in range(DP):
            xsv = xslt[0]
            xsb = (B("Bxs0"),)
            dma("sp", xsv(), d_x1.ap()[k * 128:(k + 1) * 128, c * CH:(c + 1) * CH],
                f"B0x{c}_{k}", r=xsb, w=xsb)
            rmk = "rm0" if k % 2 == 0 else "rm1"
            sc.op("dve", lambda e, k=k, rmk=rmk, xsv=xsv: e.tensor_mul(
                sb[rmk][:], xsv(), sb["bc1"][:]),
                r=xsb + (B("bc1"), B(rmk)), w=(B(rmk),), tag=f"B0l0{c}_{k}")
            sc.op("dve", lambda e, k=k, rmk=rmk, xlv=xlv: e.tensor_sub(
                xlv()[:, k * CH:(k + 1) * CH], sb[rmk][:], sb["bc2"][:]),
                r=(B(rmk), B("bc2"), xl), w=(xl,), tag=f"B0l1{c}_{k}")
            sc.op("pe", lambda e, k=k, xlv=xlv: e.matmul(
                psA[0:NH, 1024:1024 + CH], wdt()[:, k * NH:(k + 1) * NH],
                xlv()[:, k * CH:(k + 1) * CH], start=(k == 0), stop=(k == DP - 1)),
                r=(xl, B("wdt")), w=(pqA[2],), tag=f"B0d{c}_{k}")
        dma("act", d_xln.ap().rearrange("(a p) t -> p a t", p=128)[:, :, c * CH:(c + 1) * CH],
            xlv().rearrange("p (a t) -> p a t", a=DP), f"B0w{c}",
            r=(xl,), w=(B(f"B0st{c}"),))
        sc.op("act", lambda e: e.activation(sb["bc0"][0:NH, :], psA[0:NH, 1024:1024 + CH],
                                            AF.Abs, bias=sb["dtb"][:]),
              r=(pqA[2], B("dtb"), B("bc0")), w=(B("bc0"),), tag=f"B0A{c}")
        sc.op("act", lambda e: e.activation(sb["bc1"][0:NH, :], sb["bc0"][0:NH, :],
                                            AF.Exp, scale=-1.0),
              r=(B("bc0"), B("bc1")), w=(B("bc1"),), tag=f"B0E{c}")
        sc.op("act", lambda e: e.activation(sb["bc0"][0:NH, :], sb["bc1"][0:NH, :],
                                            AF.Ln, bias=sb["onescf"][0:NH, 0:1]),
              r=(B("bc1"), B("bc0"), B("onescf")), w=(B("bc0"),), tag=f"B0L{c}")
        sc.op("act", lambda e: e.activation(sb["bc1"][0:NH, :], psA[0:NH, 1024:1024 + CH],
                                            AF.Relu, bias=sb["dtb"][:]),
              r=(pqA[2], B("dtb"), B("bc1")), w=(B("bc1"),), tag=f"B0R{c}")
        sc.op("dve", lambda e, c=c: e.tensor_add(dtr()[:, c * CH:(c + 1) * CH],
                                                 sb["bc0"][0:NH, :], sb["bc1"][0:NH, :]),
              r=(B("bc0"), B("bc1"), B("dtr")), w=(B("dtr"),), tag=f"B0T{c}")

    sc.op("dve", lambda e: e.tensor_scalar(atr()[:, :], dtr()[:, :], sb["negA"][0:NH, 0:1],
                                           None, ALU.mult),
          r=(B("dtr"), B("negA"), B("atr")), w=(B("atr"),), tag="Ba")
    sc.op("dve", lambda e: e.memset(packL()[0:16, :], 0.0), w=(B("packd"),), tag="Bz")
    sc.op("dve", lambda e: e.tensor_tensor_scan(
        srows()[:, :], atr()[:, :], packL()[0:16, :], 0.0, ALU.add, ALU.add),
        r=(B("atr"), B("packd"), B("srows")), w=(B("srows"),), tag="Bscan")
    sc.op("act", lambda e: e.activation(atr()[:, :], dtr()[:, :], AF.Ln),
          r=(B("dtr"), B("atr")), w=(B("atr"),), tag="Bln")
    sc.op("dve", lambda e: e.tensor_sub(packL()[0:16, :], atr()[:, :], srows()[:, :]),
          r=(B("atr"), B("srows"), B("packd")), w=(B("packd"),), tag="Bd17")
    sc.op("dve", lambda e: e.memset(bv(SREG, 64, NH * 128), 0.0),
          w=(B("S0"),), tag="BS0")
    if dump == "B1":
        dma("act", d_dbg.ap()[0:16, 0:2048], dtr(), "dmpd", r=(B("dtr"),), w=(B("dbgw"),))
        dma("act", d_dbg.ap()[0:16, 2048:4096], srows()[0:16, :], "dmps",
            r=(B("srows"),), w=(B("dbgw"),))
        dma("act", d_dbg.ap()[0:17, 4096:6144], packL(), "dmpl",
            r=(B("packd"), B("packd0")), w=(B("dbgw"),))
    sc.barrier()

    SGEN = [0]

    def mamba_chunk(c):
        xl = B("Bxln0")
        dma("gp", xlnw().rearrange("p (a t) -> p a t", a=DP),
            d_xln.ap().rearrange("(a p) t -> p a t", p=128)[:, :, c * CH:(c + 1) * CH],
            f"B1xl{c}", r=(xl, B(f"B0st{c}")), w=(xl,))
        ZGS = [B(f"B1z{o}") for o in range(EIP)]
        XBS = [B(f"B1b{j}") for j in range(CVP2)]
        for o in range(ZXP + 1):
            pq = (pqA + pqB)[o % 8]
            ps = psA if (o % 8) < 4 else psB
            slt = (o % 8) % 4
            half_tile = o >= 32
            np_ = 64 if half_tile else 128
            wcol = (4096 + (o - 32) * 64) if half_tile else o * 128
            for k in range(DP):
                sc.op("pe", lambda e, ps=ps, slt=slt, k=k, np_=np_, wcol=wcol: e.matmul(
                    ps[0:np_, slt * CH:(slt + 1) * CH],
                    wips()[:, (k * ZXB + wcol):(k * ZXB + wcol + np_)],
                    xlnw()[:, k * CH:(k + 1) * CH],
                    start=(k == 0), stop=(k == DP - 1)),
                    r=(xl,) + WIPS, w=(pq,), tag=f"B1i{c}_{o}_{k}")
            if o < EIP:
                sc.op("act", lambda e, ps=ps, slt=slt, o=o: e.activation(
                    zg()[:, o * CH:(o + 1) * CH], ps[:, slt * CH:(slt + 1) * CH], AF.Copy),
                    r=(pq, ZGS[o]), w=(ZGS[o],), tag=f"B1ze{c}_{o}")
            else:
                j = o - EIP
                if c == 0:
                    sc.op("dve", lambda e, j=j, np_=np_: e.memset(
                        xbc()[0:np_, j * XSL:j * XSL + 3], 0.0),
                        r=(XBS[j],), w=(XBS[j],), tag=f"B1t{c}_{j}")
                else:
                    sc.op("dve", lambda e, j=j, np_=np_: e.tensor_copy(
                        xbc()[0:np_, j * XSL:j * XSL + 3],
                        xbc()[0:np_, j * XSL + CH:j * XSL + CH + 3]),
                        r=(XBS[j],), w=(XBS[j],), tag=f"B1t{c}_{j}")
                sc.op("act", lambda e, ps=ps, slt=slt, j=j, np_=np_: e.activation(
                    xbc()[0:np_, j * XSL + 3:j * XSL + 3 + CH],
                    ps[0:np_, slt * CH:(slt + 1) * CH], AF.Copy),
                    r=(pq, XBS[j]), w=(XBS[j],), tag=f"B1xe{c}_{j}")
        if dump == "B1" and c == 0:
            dma("gp", d_dbg.ap()[:, 6144:6144 + CVP2 * XSL], xbc(), "dmpx",
                r=tuple(XBS), w=(B("dbgw"),))
        EQ = B("B1eq")

        def esq_build(e, c=c):
            sr = srows()
            pst = sr.ap[0][0]
            t, off = sr.tensor, sr.offset
            lastap = bass.AP(t, off + c * CH + Q - 1, [[pst, NH], [Q, NSUB]])
            if c == 0:
                e.tensor_sub(sb["bc0"][0:NH, 1:NSUB],
                             bass.AP(t, off + 2 * Q - 1, [[pst, NH], [Q, NSUB - 1]]),
                             bass.AP(t, off + Q - 1, [[pst, NH], [Q, NSUB - 1]]))
                return e.tensor_copy(sb["bc0"][0:NH, 0:1],
                                     bass.AP(t, off + Q - 1, [[pst, NH], [1, 1]]))
            prevap = bass.AP(t, off + c * CH - 1, [[pst, NH], [Q, NSUB]])
            return e.tensor_sub(sb["bc0"][0:NH, 0:NSUB], lastap, prevap)
        sc.op("dve", esq_build, r=(B("srows"), B("bc0")), w=(B("bc0"),), tag=f"B1q0{c}")
        sc.op("act", lambda e: e.activation(sb["bc1"][0:NH, 0:NSUB], sb["bc0"][0:NH, 0:NSUB], AF.Exp),
              r=(B("bc0"), B("bc1")), w=(B("bc1"),), tag=f"B1q1{c}")
        sc.op("gp", lambda e: e.dma_start(out=sb["rowst"][0:1, 0:NH * NSUB],
                                          in_=sb["bc1"][0:NH, 0:NSUB]),
              r=(B("bc1"), B("rowst")), w=(B("rowst"),), dma="gp", tag=f"B1qf{c}")
        sc.op("pe", lambda e: e.matmul(psB[0:64, 1024:1024 + NH * NSUB], sb["ones1"][0:1, 0:64],
                                       sb["rowst"][0:1, 0:NH * NSUB], start=True, stop=True),
              r=(B("rowst"), B("ones1")), w=(pqB[2],), tag=f"B1qb{c}")
        sc.op("dve", lambda e: e.tensor_copy(esq(), psB[0:64, 1024:1024 + NH * NSUB]),
              r=(pqB[2],), w=(EQ,), tag=f"B1qc{c}")

        for q in range(NSUB):
            half, qh = q // 2, (q % 2) * 128
            CVB = B("B1cv")
            if q % 2 == 0:
                for j in range(CVP2):
                    np_ = 128 if j < 16 else 64
                    base = j * XSL + half * 256
                    sc.op("dve", lambda e, base=base, j=j, np_=np_: e.tensor_scalar(
                        sb["tmpc"][0:np_, 0:256], xbc()[0:np_, base:base + 256],
                        sb["acw"][0:np_, 4 * j:4 * j + 1], None, ALU.mult),
                        r=(XBS[j], B("acw"), B("tm0")), w=(B("tm0"),), tag=f"B1v0{c}_{q}_{j}")
                    for kk in (1, 2, 3):
                        src_t = "tm0" if kk % 2 == 1 else "tm1"
                        dst_t = "tm1" if kk % 2 == 1 else "tm0"
                        sc.op("dve", lambda e, base=base, j=j, kk=kk, np_=np_: e.scalar_tensor_tensor(
                            sb["tmpc"][0:np_, (kk % 2) * CH:(kk % 2) * CH + 256],
                            xbc()[0:np_, base + kk:base + kk + 256],
                            sb["acw"][0:np_, 4 * j + kk:4 * j + kk + 1],
                            sb["tmpc"][0:np_, ((kk + 1) % 2) * CH:((kk + 1) % 2) * CH + 256],
                            ALU.mult, ALU.add),
                            r=(XBS[j], B(src_t), B("acw"), B(dst_t)), w=(B(dst_t),),
                            tag=f"B1v{kk}{c}_{q}_{j}")
                    cvdep = (B(f"B0st{NCH - 1}"),) if c == 0 else ()
                    sc.op("act", lambda e, j=j, np_=np_: e.activation(
                        cvh()[0:np_, j * 256:(j + 1) * 256], sb["tmpc"][0:np_, CH:CH + 256],
                        AF.Silu, bias=sb["acb"][0:np_, j:j + 1]),
                        r=(B("tm1"), B("acb"), CVB) + cvdep, w=(CVB,), tag=f"B1vs{c}_{q}_{j}")
            sc.op("pe", lambda e, qh=qh: e.matmul(
                psB[:, 0:128], cvh()[0:64, 16 * 256 + qh:16 * 256 + qh + 128],
                cvh()[0:64, 17 * 256 + qh:17 * 256 + qh + 128], start=True, stop=True),
                r=(CVB,), w=(pqB[0],), tag=f"B1M{c}_{q}")
            sc.op("dve", lambda e: e.tensor_copy(sb["Msb"][:], psB[:, 0:128]),
                  r=(pqB[0], B("Msb")), w=(B("Msb"),), tag=f"B1Me{c}_{q}")
            sc.op("pe", lambda e, qh=qh: e.transpose(
                psBb()[0:128, 256:320], cvh()[0:64, 16 * 256 + qh:16 * 256 + qh + 128],
                sb["idb"][0:64, 0:64]),
                r=(CVB, B("idb")), w=(pqB[0],), tag=f"B1bt{c}_{q}")
            sc.op("dve", lambda e: e.tensor_copy(btm(), psBb()[0:128, 256:320]),
                  r=(pqB[0], B("btm")), w=(B("btm"),), tag=f"B1be{c}_{q}")
            if c == 0 and q == 0:
                sc.op("dve", lambda e: e.memset(lex()[0:16, :], 0.0),
                      r=(B("lexr"),), w=(B("lexr"),), tag="B1x00")
            else:
                def lexf(e, c=c, q=q):
                    sr = srows()
                    col = bass.AP(sr.tensor, sr.offset + c * CH + q * Q - 1,
                                  [[sr.ap[0][0], NH], [0, 64]])
                    return e.tensor_scalar(lex()[0:16, :], col, -1.0, None, ALU.mult)
                sc.op("dve", lexf, r=(B("srows"), B("lexr")), w=(B("lexr"),), tag=f"B1x{c}_{q}")

            def sflat(e, c=c, q=q):
                sr = srows()
                src = bass.AP(sr.tensor, sr.offset + c * CH + q * Q, [[sr.ap[0][0], NH], [1, Q]])
                return e.dma_start(out=packR()[16:17, :], in_=src)
            sc.op("gp", sflat, r=(B("srows"), B("packs")), w=(B("packs"),), dma="gp",
                  tag=f"B1f{c}_{q}")
            for j in range(4):
                sc.op("pe", lambda e, j=j, c=c, q=q: e.matmul(
                    psA[:, j * 512:(j + 1) * 512],
                    packL()[0:17, c * CH + q * Q:c * CH + q * Q + Q],
                    packR()[0:17, j * 512:(j + 1) * 512], start=True, stop=False),
                    r=(B("packs"), B("packd"), B("packind"), B("packd0")),
                    w=(pqA[j],), tag=f"B1E{c}_{q}_{j}")
                sc.op("pe", lambda e, j=j: e.matmul(
                    psA[:, j * 512:(j + 1) * 512], sb["mnegTb"][:],
                    sb["idrepb"][:, j * 512:(j + 1) * 512], start=False, stop=True),
                    r=(B("mnegTb"), B("idrepb")), w=(pqA[j],), tag=f"B1G{c}_{q}_{j}")
            sc.op("act", lambda e: e.activation(Esb(), psA[:, 0:2048], AF.Exp),
                  r=tuple(pqA) + (B("Esb"),), w=(B("Esb"),), tag=f"B1X{c}_{q}")
            sc.op("act", lambda e: e.activation(
                sb["dcol"][:], bass.AP(psA.ap().tensor, psA.ap().offset + 127,
                                       [[psA.ap().ap[0][0], 128], [128, NH]]), AF.Exp),
                r=tuple(pqA) + (B("dcol"),), w=(B("dcol"),), tag=f"B1dc{c}_{q}")
            for j in range(4):
                sc.op("pe", lambda e, j=j: e.matmul(
                    psB[0:64, j * 512:(j + 1) * 512], lex(),
                    packR()[0:17, j * 512:(j + 1) * 512], start=True, stop=True),
                    r=(B("packs"), B("packind"), B("lexr"), B("lex0")),
                    w=(pqB[j],), tag=f"B1H{c}_{q}_{j}")
            sc.op("act", lambda e: e.activation(psA[0:64, 0:2048], psB[0:64, 0:2048], AF.Exp),
                  r=tuple(pqB) + tuple(pqA), w=tuple(pqA), tag=f"B1Y{c}_{q}")
            sc.op("dve", lambda e: e.tensor_mul(
                Wpp().rearrange("p (a t) -> p a t", t=128),
                Esb().rearrange("p (a t) -> p a t", t=128),
                sb["Msb"][:].unsqueeze(1).broadcast_to((128, NH, 128))),
                r=(B("Esb"), B("Msb"), B("Wpp")), w=(B("Wpp"),), tag=f"B1W{c}_{q}")
            sc.op("dve", lambda e, qh=qh: e.tensor_mul(
                Wext().rearrange("p (a t) -> p a t", t=128),
                psA[0:64, 0:2048].rearrange("p (a t) -> p a t", t=128),
                cvh()[0:64, 17 * 256 + qh:17 * 256 + qh + 128].unsqueeze(1)
                .broadcast_to((64, NH, 128))),
                r=tuple(pqA) + (CVB, B("Wext")), w=(B("Wext"),), tag=f"B1Wx{c}_{q}")

            for h in range(NH):
                sc.op("pe", lambda e, h=h, qh=qh: e.transpose(
                    psBb()[0:128, (h % 4) * 1024:(h % 4) * 1024 + 128],
                    cvh()[:, h * 256 + qh:h * 256 + qh + 128], sb["idb"][:]),
                    r=(CVB, B("idb")), w=(pqB[h % 4],), tag=f"B1T{c}_{q}_{h}")
                sc.op("dve", lambda e, h=h: e.tensor_copy(
                    Xt()[:, h * 128:(h + 1) * 128],
                    psBb()[0:128, (h % 4) * 1024:(h % 4) * 1024 + 128]),
                    r=(pqB[h % 4], B("XtB")), w=(B("XtB"),), tag=f"B1Te{c}_{q}_{h}")
            sg = SGEN[0]
            for h in range(NH):
                sc.op("pe", lambda e, h=h: e.matmul(
                    psA[:, h * 128:(h + 1) * 128], Xt()[:, h * 128:(h + 1) * 128],
                    Wpp()[:, h * 128:(h + 1) * 128], start=True, stop=False),
                    r=(B("XtB"), B("Wpp")), w=(pqA[h // 4],), tag=f"B1y0{c}_{q}_{h}")
                sc.op("pe", lambda e, h=h: e.matmul(
                    psA[:, h * 128:(h + 1) * 128], Xt()[:, h * 128:(h + 1) * 128],
                    sb["DI"][:, h * 128:(h + 1) * 128], start=False, stop=False),
                    r=(B("XtB"), B("DI")), w=(pqA[h // 4],), tag=f"B1y1{c}_{q}_{h}")
                sc.op("pe", lambda e, h=h, sg=sg: e.matmul(
                    psA[:, h * 128:(h + 1) * 128], sview(sg, h),
                    Wext()[:, h * 128:(h + 1) * 128], start=False, stop=True),
                    r=(B(f"S{sg}"), B("Wext")), w=(pqA[h // 4],), tag=f"B1y2{c}_{q}_{h}")
            zgst = zg()
            sc.op("act", lambda e, q=q: e.activation(
                zsil().rearrange("p (a t) -> p a t", t=128),
                bass.AP(zgst.tensor, zgst.offset + q * Q,
                        [[zgst.ap[0][0], 128], [CH, NH], [1, Q]]), AF.Silu),
                r=tuple(ZGS) + (B("zsil"),), w=(B("zsil"),), tag=f"B1zs{c}_{q}")
            for h in range(NH):
                sc.op("dve", lambda e, h=h, q=q: e.scalar_tensor_tensor(
                    bass.AP(zgst.tensor, zgst.offset + h * CH + q * Q,
                            [[zgst.ap[0][0], 128], [1, Q]]),
                    psA[:, h * 128:(h + 1) * 128], sb["nwcol"][:, h:h + 1],
                    zsil()[:, h * 128:(h + 1) * 128], ALU.mult, ALU.mult),
                    r=(pqA[h // 4], B("nwcol"), B("zsil"), ZGS[h]),
                    w=(ZGS[h],), tag=f"B1g{c}_{q}_{h}")
            for h in range(NH):
                bd = B(f"bd{h % 2}")
                sc.op("dve", lambda e, h=h: e.tensor_scalar(
                    bdec()[:, (h % 2) * DS:(h % 2 + 1) * DS], btm(),
                    sb["dcol"][:, h:h + 1], None, ALU.mult),
                    r=(B("btm"), B("dcol"), bd), w=(bd,), tag=f"B1bd{c}_{q}_{h}")
                sc.op("pe", lambda e, h=h: e.matmul(
                    psB[0:64, h * 128:(h + 1) * 128], bdec()[:, (h % 2) * DS:(h % 2 + 1) * DS],
                    Xt()[:, h * 128:(h + 1) * 128], start=True, stop=True),
                    r=(bd, B("XtB")), w=(pqB[h // 4],), tag=f"B1u0{c}_{q}_{h}")
            for h in range(NH):
                sc.op("dve", lambda e, h=h, sg=sg, q=q: e.scalar_tensor_tensor(
                    sview(1 - sg, h), sview(sg, h), esq()[:, h * NSUB + q:h * NSUB + q + 1],
                    psB[0:64, h * 128:(h + 1) * 128], ALU.mult, ALU.add),
                    r=(B(f"S{sg}"), EQ, pqB[h // 4], B(f"S{1 - sg}")),
                    w=(B(f"S{1 - sg}"),), tag=f"B1u1{c}_{q}_{h}")
            SGEN[0] = 1 - sg

        for k in range(EIP):
            sq = B(f"B1r{k % 2}")
            sc.op("act", lambda e, k=k: e.activation(
                sb["sqpp"][:, (k % 2) * CH:(k % 2 + 1) * CH],
                zg()[:, k * CH:(k + 1) * CH], AF.Square),
                r=(ZGS[k], sq), w=(sq,), tag=f"B1s2{c}_{k}")
            sc.op("pe", lambda e, k=k: e.matmul(
                psA[0:1, 0:CH], sb["inw2"][:, k:k + 1],
                sb["sqpp"][:, (k % 2) * CH:(k % 2 + 1) * CH],
                start=(k == 0), stop=(k == EIP - 1)),
                r=(sq, B("inw2")), w=(pqA[0],), tag=f"B1sm{c}_{k}")
        rs = B("rowst")
        sc.op("dve", lambda e: e.tensor_copy(sb["rowst"][:], psA[0:1, 0:CH]),
              r=(pqA[0], rs), w=(rs,), tag=f"B1rc{c}")
        sc.op("pe", lambda e: e.matmul(psB[:, 0:CH], sb["ones1"][:], sb["rowst"][:],
                                       start=True, stop=True),
              r=(rs, B("ones1")), w=(pqB[0],), tag=f"B1rb{c}")
        sc.op("dve", lambda e: e.tensor_scalar(sb["bc0"][:], psB[:, 0:CH], 1.0 / EI, 1e-5,
                                               ALU.mult, ALU.add),
              r=(pqB[0], B("bc0")), w=(B("bc0"),), tag=f"B1r0{c}")
        sc.op("act", lambda e: e.activation(sb["bc1"][:], sb["bc0"][:], AF.Sqrt),
              r=(B("bc0"), B("bc1")), w=(B("bc1"),), tag=f"B1r1{c}")
        sc.op("dve", lambda e: e.reciprocal(sb["bc2"][:], sb["bc1"][:]),
              r=(B("bc1"), B("bc2")), w=(B("bc2"),), tag=f"B1r2{c}")
        for k in range(EIP):
            wk = B(f"B1w{k % 2}")
            dma("gp", wkwin()[:, (k % 2) * 1024:(k % 2 + 1) * 1024],
                d_wopt.ap()[k * 128:(k + 1) * 128, :], f"B1o{c}_{k}", r=(wk,), w=(wk,))
            for o in range(8):
                ps = psA if o < 4 else psB
                pq = pqA[o % 4] if o < 4 else pqB[o % 4]
                sc.op("pe", lambda e, ps=ps, o=o, k=k: e.matmul(
                    ps[:, (o % 4) * CH:(o % 4 + 1) * CH],
                    wkwin()[:, (k % 2) * 1024 + o * 128:(k % 2) * 1024 + o * 128 + 128],
                    zg()[:, k * CH:(k + 1) * CH], start=(k == 0), stop=(k == EIP - 1)),
                    r=(wk, ZGS[k]), w=(pq,), tag=f"B1p{c}_{o}_{k}")
        for o in range(8):
            ps = psA if o < 4 else psB
            pq = pqA[o % 4] if o < 4 else pqB[o % 4]
            xsv = xslt[0]
            xsb = (B("Bxs0"),)
            dma("sp", xsv(), d_x1.ap()[o * 128:(o + 1) * 128, c * CH:(c + 1) * CH],
                f"B1x2{c}_{o}", r=xsb, w=xsb)
            rmk = "rm0" if o % 2 == 0 else "rm1"
            ot = "bc3" if o % 2 == 0 else "bc1"
            sc.op("dve", lambda e, ps=ps, o=o, rmk=rmk: e.tensor_mul(
                sb[rmk][:], ps[:, (o % 4) * CH:(o % 4 + 1) * CH], sb["bc2"][:]),
                r=(pq, B("bc2"), B(rmk)), w=(B(rmk),), tag=f"B1e0{c}_{o}")
            sc.op("dve", lambda e, o=o, rmk=rmk, ot=ot, xsv=xsv: e.scalar_tensor_tensor(
                sb[ot][:], sb[rmk][:], 0.0, xsv(), ALU.bypass, ALU.add),
                r=(B(rmk),) + xsb + (B(ot),), w=(B(ot),), tag=f"B1e1{c}_{o}")
            dma("act", d_x2.ap()[o * 128:(o + 1) * 128, c * CH:(c + 1) * CH],
                sb[ot][:], f"B1st{c}_{o}", r=(B(ot),), w=(B(f"B1d{c}_{o}"),))

    for c in range(NCH):
        mamba_chunk(c)
    sc.barrier()
    if phases == "AB":
        dma("sp", d_out.ap(), d_x2.ap(), "cpout", r=(), w=())
        sc.finalize()
        _emit(nc, sc, sems, FILLSCR)
        return nc

    for j in range(4):
        dma("sp", wbv(j * 16384, 128, DP * HF // 4),
            d_w1q.ap()[:, j * (DP * HF // 4):(j + 1) * (DP * HF // 4)],
            f"w1r{j}", r=(W1S[j],), w=(W1S[j],))
    ffn_phase("C", d_x2, d_out)

    sc.finalize()
    _emit(nc, sc, sems, FILLSCR)
    return nc


def _emit(nc, sc, sems, fillscr=None):
    fills = {}
    if fillscr is not None:
        fills = {
            "dve": lambda e: e.tensor_copy(fillscr[0:1, 0:1], fillscr[0:1, 1:2]),
            "act": lambda e: e.activation(fillscr[0:1, 2:3], fillscr[0:1, 3:4], AF.Copy),
        }
    with nc.Block() as block:
        def run(eng_name):
            def body(eng):
                sc.emit(eng_name, eng, sems, fills.get(eng_name))
                if eng_name == "sp":
                    for qn, tot in sc.dma_totals.items():
                        eng.wait_ge(sems["d" + qn], tot)
            return body
        block.sync(run("sp"))
        block.gpsimd(run("gp"))
        block.tensor(run("pe"))
        block.vector(run("dve"))
        block.scalar(run("act"))


_NC_CACHE = [None]


def kernel(**inputs):
    f32 = np.float32

    def cc(a):
        return np.ascontiguousarray(np.asarray(a), dtype=f32)

    u = np.asarray(inputs["L0_attn_u"])
    ip = np.asarray(inputs["L0_attn_in_proj_w"])
    base = dict(
        w1t=cc(np.asarray(inputs["L0_ffn_w1"]).T),
        w2t=cc(np.asarray(inputs["L0_ffn_w2"]).T),
        wipt=cc(ip[:ZXB].T),
        wiptdt=cc(ip[ZXB:].T),
        wopt=cc(np.asarray(inputs["L0_attn_out_proj_w"]).T),
        fcw=cc(np.asarray(inputs["L0_ffn_conv_w"]).reshape(HF, 3)),
        fcb=cc(np.asarray(inputs["L0_ffn_conv_b"]).reshape(HF, 1)),
        acw=cc(np.concatenate([np.asarray(inputs["L0_attn_conv_w"]).reshape(CONVD, 4),
                               np.asarray(inputs["L0_attn_conv_w"]).reshape(CONVD, 4)[2112:2176],
                               np.zeros((64, 4), np.float32)])),
        acb=cc(np.concatenate([np.asarray(inputs["L0_attn_conv_b"]).reshape(CONVD, 1),
                               np.asarray(inputs["L0_attn_conv_b"]).reshape(CONVD, 1)[2112:2176],
                               np.zeros((64, 1), np.float32)])),
        dtb=cc(np.asarray(inputs["L0_attn_dt_bias"]).reshape(NH, 1)),
        alog=cc(np.asarray(inputs["L0_attn_A_log"]).reshape(NH, 1)),
        drow=cc(np.repeat(np.asarray(inputs["L0_attn_D_param"]), 128).reshape(1, EI)),
        nwc=cc(np.asarray(inputs["L0_attn_norm_w"]).reshape(1, EI)),
        ident=np.eye(128, dtype=f32),
        mnegT=np.where(np.arange(128)[:, None] >= np.arange(128)[None, :],
                       np.float32(0.0), np.float32(NEG)).astype(f32),
        indic=np.kron(np.eye(NH, dtype=f32), np.ones((1, 128), dtype=f32)),
        onesr=np.ones((1, T), dtype=f32),
    )
    in_maps = [dict(base, u=cc(u[i].T)) for i in range(8)]
    if _NC_CACHE[0] is None:
        _NC_CACHE[0] = build()
    res = run_bass_kernel_spmd(_NC_CACHE[0], in_maps, core_ids=list(range(8)))
    out = np.stack([np.asarray(r["out"]).T for r in res.results], axis=0)
    return out.astype(np.float32)
